# version 02 (final)
# baseline (speedup 1.0000x reference)
import contextlib
from collections import defaultdict

import numpy as np
import ml_dtypes
import concourse.bass as bass
import concourse.mybir as mybir
from concourse.bass_utils import run_bass_kernel_spmd

F32 = mybir.dt.float32
BF16 = mybir.dt.bfloat16
I32 = mybir.dt.int32
AF = mybir.ActivationFunctionType
ALU = mybir.AluOpType
AX = mybir.AxisListType

NCORES = 8
T = 2048
TT = T // 128
TG = T // 512
D = 1024
DC = D // 128

ENGS = "TVSPQ"

STEPS = [1, 2, 3, 4, 5, 6, 7, 8]
MCUT = 99
JDBG = False
FDBG = False

LN2 = float(np.log(2.0))
PI = float(np.pi)
TWO_PI_HI = np.float32(2.0 * np.pi)
TWO_PI_LO = np.float32(2.0 * np.pi - float(np.float32(2.0 * np.pi)))



class Op:
    __slots__ = ("eng", "emit", "deps", "is_dma", "needs_inc", "val", "sem", "seq")

    def __init__(self, eng, emit, deps, is_dma, seq):
        self.eng = eng
        self.emit = emit
        self.deps = deps
        self.is_dma = is_dma
        self.needs_inc = False
        self.val = 0
        self.sem = None
        self.seq = seq


class Buf:
    __slots__ = ("w", "rs", "name")

    def __init__(self, name=""):
        self.w = None
        self.rs = []
        self.name = name


class Prog:
    def __init__(self, nc):
        self.nc = nc
        self.ops = {k: [] for k in ENGS}
        self.pending = {k: [] for k in ENGS}
        self.seq = 0
        self.eng_obj = {"T": nc.tensor, "V": nc.vector, "S": nc.scalar,
                        "P": nc.gpsimd, "Q": nc.sync}

    def op(self, eng, emit, reads=(), writes=(), dma=False):
        deps = []
        for b in reads:
            if b.w is not None:
                deps.append(b.w)
        for b in writes:
            if b.w is not None:
                deps.append(b.w)
            deps.extend(b.rs)
        if self.pending[eng]:
            deps.extend(self.pending[eng])
            self.pending[eng] = []
        o = Op(eng, emit, deps, dma, self.seq)
        self.seq += 1
        self.ops[eng].append(o)
        for b in writes:
            b.w = o
            b.rs = []
        for b in reads:
            b.rs.append(o)
        return o

    def barrier(self):
        lasts = [self.ops[e][-1] for e in ENGS if self.ops[e]]
        for e in ENGS:
            self.pending[e] = list(lasts)

    def build(self, ctx):
        nc = self.nc
        for e in ENGS:
            for o in self.ops[e]:
                if o.is_dma:
                    o.needs_inc = True
                for d in o.deps:
                    if d.eng != o.eng or d.is_dma:
                        d.needs_inc = True
        used_sems = set()
        for e in ENGS:
            c_comp = 0
            c_dma = 0
            for o in self.ops[e]:
                if o.needs_inc:
                    if o.is_dma:
                        c_dma += 16
                        o.val = c_dma
                        o.sem = e + "d"
                    else:
                        c_comp += 1
                        o.val = c_comp
                        o.sem = e + "c"
                    used_sems.add(o.sem)
        sems = {s: ctx.enter_context(nc.semaphore(s)) for s in sorted(used_sems)}
        waited = {e: defaultdict(int) for e in ENGS}

        def make_runner(e):
            def run(eng_obj):
                for o in self.ops[e]:
                    need = {}
                    for d in o.deps:
                        if d.eng == o.eng and not d.is_dma:
                            continue
                        assert d.seq < o.seq, "forward dependency"
                        if d.val > need.get(d.sem, 0):
                            need[d.sem] = d.val
                    for s in sorted(need):
                        v = need[s]
                        if waited[e][s] < v:
                            eng_obj.wait_ge(sems[s], v)
                            waited[e][s] = v
                    inst = o.emit()
                    if o.needs_inc:
                        inst.then_inc(sems[o.sem], 16 if o.is_dma else 1)
            return run

        block = ctx.enter_context(nc.Block())
        block.tensor(make_runner("T"))
        block.vector(make_runner("V"))
        block.scalar(make_runner("S"))
        block.gpsimd(make_runner("P"))
        block.sync(make_runner("Q"))

    def dma(self, q, out, in_, reads=(), writes=(), slow=False):
        eng = self.eng_obj[q]
        return self.op(q, lambda: eng.dma_start(
            out=out, in_=in_, allow_slow_non_contiguous=slow),
            reads=reads, writes=writes, dma=True)

    def dmat(self, q, out, in_, reads=(), writes=()):
        eng = self.eng_obj[q]
        return self.op(q, lambda: eng.dma_start_transpose(out, in_),
                       reads=reads, writes=writes, dma=True)

    def mm(self, out, lhsT, rhs, start, stop, reads=(), writes=(), tp=None):
        nc = self.nc
        return self.op("T", lambda: nc.tensor.matmul(out, lhsT, rhs,
                                                     start=start, stop=stop,
                                                     tile_position=tp),
                       reads=reads, writes=writes)

    def tr(self, out, in_, ident_ap, reads=(), writes=()):
        nc = self.nc
        return self.op("T", lambda: nc.tensor.transpose(out, in_, ident_ap),
                       reads=reads, writes=writes)

    def v(self, fname, *args, reads=(), writes=(), **kw):
        f = getattr(self.nc.vector, fname)
        return self.op("V", lambda: f(*args, **kw), reads=reads, writes=writes)

    def s(self, fname, *args, reads=(), writes=(), **kw):
        f = getattr(self.nc.scalar, fname)
        return self.op("S", lambda: f(*args, **kw), reads=reads, writes=writes)

    def g(self, fname, *args, reads=(), writes=(), **kw):
        f = getattr(self.nc.gpsimd, fname)
        return self.op("P", lambda: f(*args, **kw), reads=reads, writes=writes)

    def sep(self, *bufs):
        nc = self.nc
        return self.op("P", lambda: nc.gpsimd.engine_nop(),
                       reads=tuple(bufs), writes=tuple(bufs))


class Tile:
    __slots__ = ("t", "off", "cols", "b", "arena")

    def __init__(self, t, off, cols, arena):
        self.t = t
        self.off = off
        self.cols = cols
        self.b = Buf()
        self.arena = arena

    def a(self, c0=0, c1=None, p0=0, p1=128):
        if c1 is None:
            c1 = self.cols
        assert 0 <= c0 <= c1 <= self.cols
        return self.t[p0:p1, self.off + c0: self.off + c1]

    def free(self):
        self.arena.free(self)


class Arena:
    def __init__(self, t, cols, name):
        self.t = t
        self.cols = cols
        self.name = name
        self.spans = [(0, cols)]
        self.hazards = []
        self.hwm = 0

    def alloc(self, cols, align=16):
        cols = (cols + align - 1) // align * align
        for i, (o, n) in enumerate(self.spans):
            if cols <= n:
                self.spans[i] = (o + cols, n - cols)
                if self.spans[i][1] == 0:
                    del self.spans[i]
                self.hwm = max(self.hwm, self.cols - sum(n for _, n in self.spans))
                t = Tile(self.t, o, cols, self)
                keep = []
                for ho, hn, hops in self.hazards:
                    if ho < o + cols and o < ho + hn:
                        t.b.rs.extend(hops)
                        if ho >= o and ho + hn <= o + cols:
                            continue
                    keep.append((ho, hn, hops))
                self.hazards = keep
                return t
        raise MemoryError(f"arena {self.name}: alloc {cols} failed "
                          f"(free={sum(n for _, n in self.spans)})")

    def free(self, tile):
        ops = list(tile.b.rs)
        if tile.b.w is not None:
            ops.append(tile.b.w)
        if ops:
            self.hazards.append((tile.off, tile.cols, ops))
        self.spans.append((tile.off, tile.cols))
        self.spans.sort()
        merged = []
        for o, n in self.spans:
            if merged and merged[-1][0] + merged[-1][1] == o:
                merged[-1] = (merged[-1][0], merged[-1][1] + n)
            else:
                merged.append([o, n])
        self.spans = [tuple(x) for x in merged]


class PsumPool:
    def __init__(self, nc, ctx):
        self.banks = []
        for i in range(8):
            t = ctx.enter_context(nc.psum_tensor(f"pb{i}", [128, 512], F32))
            self.banks.append((t, Buf(f"pb{i}")))
        self.i = 0
        self.pinned = set()

    def nxt(self):
        for _ in range(8):
            j = self.i % 8
            self.i += 1
            if j not in self.pinned:
                return self.banks[j]
        raise RuntimeError("all psum banks pinned")

    def pin(self, n):
        out = []
        for j in range(8):
            if len(out) < n and j not in self.pinned:
                self.pinned.add(j)
                out.append((j, self.banks[j][0], self.banks[j][1]))
        assert len(out) == n
        return out

    def unpin(self, pins):
        for j, _, _ in pins:
            self.pinned.discard(j)



class Env:
    def __init__(self, nc, iota=False):
        self.nc = nc
        self.P = Prog(nc)
        self.ctx = contextlib.ExitStack()
        self.used_inputs = []
        self.out_names = []
        self.store_ops = []
        AB_COLS = 61984
        AF_COLS = 21152
        self.AB = Arena(self.ctx.enter_context(
            nc.sbuf_tensor("abf", [128, AB_COLS], BF16)), AB_COLS, "bf")
        self.AF = Arena(self.ctx.enter_context(
            nc.sbuf_tensor("af32", [128, AF_COLS], F32)), AF_COLS, "f32")
        self.pp = PsumPool(nc, self.ctx)
        self.IT = self.ctx.enter_context(
            nc.sbuf_tensor("it32", [128, 1024], I32))
        self.ITb = Buf("it32")
        P = self.P
        self.ident = self.AF.alloc(128)
        self.ocf = self.AF.alloc(16)
        self.orow = self.AF.alloc(128)
        self.ocb = self.AB.alloc(16)
        self.orb = self.AB.alloc(128)
        self.SC = self.AF.alloc(16)
        self.SCAL = {1e-5: 0, 1.0: 1, -1.0: 2, 1e-6: 3, 0.0: 4, 0.5: 5,
                     1e-12: 6, 2.0: 7, -LN2: 8, -5 * LN2: 9}
        c_ident = self.uin("c_ident", [128, 128])
        c_scalars = self.uin("c_scalars", [128, 12])
        c_ones_col_b = self.uin("c_ones_col_b", [128, 1], BF16)
        c_ones_col_f = self.uin("c_ones_col_f", [128, 1])
        c_ones_row = self.uin("c_ones_row", [1, 128])
        c_ones_row_b = self.uin("c_ones_row_b", [1, 128], BF16)
        P.dma("Q", self.ident.a(), c_ident, writes=(self.ident.b,))
        P.dma("Q", self.SC.a(0, 12), c_scalars[:, 0:12], writes=(self.SC.b,))
        P.dma("Q", self.ocf.a(0, 1), c_ones_col_f, writes=(self.ocf.b,))
        P.dma("Q", self.orow.a(0, 128, 0, 1), c_ones_row, writes=(self.orow.b,))
        P.dma("Q", self.ocb.a(0, 1), c_ones_col_b, writes=(self.ocb.b,))
        P.dma("Q", self.orb.a(0, 128, 0, 1), c_ones_row_b, writes=(self.orb.b,))
        if iota:
            self.IR = self.AF.alloc(T)
            self.ICOL = self.AF.alloc(16)
            c_iota_row = self.uin("c_iota_row", [1, T])
            c_iota_col = self.uin("c_iota_col", [128, 1])
            P.dma("Q", self.IR.a(0, T, 0, 1), c_iota_row, writes=(self.IR.b,))
            P.dma("Q", self.ICOL.a(0, 1), c_iota_col, writes=(self.ICOL.b,))
        self._pmax_dram = None
        self._pmax_n = 0

    def uin(self, name, shape, dtype=F32):
        self.used_inputs.append(name)
        return self.nc.dram_tensor(name, list(shape), dtype,
                                   kind="ExternalInput").ap()

    def uout(self, name, shape, dtype=F32):
        self.out_names.append(name)
        return self.nc.dram_tensor(name, list(shape), dtype,
                                   kind="ExternalOutput").ap()

    def sc(self, v, p0=0, p1=128):
        i = self.SCAL[v]
        return self.SC.a(i, i + 1, p0, p1)

    def finish(self):
        self.P.pending["Q"].extend(self.store_ops)
        self.P.op("Q", lambda: self.nc.sync.nop())
        self.P.build(self.ctx)
        self.ctx.close()

    def replicate_row(self, src_ap, src_bufs, width, dst, dst_c0, p1=128):
        P, pp = self.P, self.pp
        ng = (width + 511) // 512
        for g in range(ng):
            w = min(512, width - g * 512)
            pt, pb = pp.nxt()
            P.mm(pt[0:p1, 0:w], self.orow.a(0, p1, 0, 1),
                 src_ap[0:1, g * 512:g * 512 + w],
                 start=True, stop=True, reads=(self.orow.b,) + tuple(src_bufs),
                 writes=(pb,))
            P.v("tensor_copy", dst.a(dst_c0 + g * 512, dst_c0 + g * 512 + w,
                                     0, p1),
                pt[0:p1, 0:w], reads=(pb,), writes=(dst.b,))

    def load_w_bf16(self, hbm_ap, rows, cols):
        assert rows % 128 == 0
        n = rows // 128
        t = self.AB.alloc(n * cols)
        self.P.dma("P", t.a().rearrange("p (a d) -> p a d", a=n),
                   hbm_ap.rearrange("(a p) d -> p a d", p=128), writes=(t.b,))
        return t

    def transpose_w(self, hbm_ap, K, M, piece=1024):
        assert K % 128 == 0 and M % 128 == 0
        P = self.P
        t = self.AB.alloc((K // 128) * M)
        piece = min(M, piece)
        for m0 in range(0, M, piece):
            raw = self.load_w_bf16(hbm_ap[m0:m0 + piece, :], piece, K)
            for mc in range(piece // 128):
                for kc in range(K // 128):
                    P.dmat("Q",
                           t.a(kc * M + m0 + mc * 128,
                               kc * M + m0 + mc * 128 + 128),
                           raw.a(mc * K + kc * 128, mc * K + kc * 128 + 128),
                           reads=(raw.b,), writes=(t.b,))
            raw.free()
        return t

    def layernorm(self, src_fn, src_bufs, dst_fn, dst_bufs, dst_dt="bf16",
                  gcol=None, bcol=None, pbufs=()):
        P, pp, AB, AFr = self.P, self.pp, self.AB, self.AF
        xb = AB.alloc(DC * T)
        sq = AB.alloc(DC * T)
        for c in range(DC):
            P.v("tensor_copy", xb.a(c * T, (c + 1) * T), src_fn(c),
                reads=tuple(src_bufs), writes=(xb.b,))
            P.s("activation", sq.a(c * T, (c + 1) * T),
                xb.a(c * T, (c + 1) * T), AF.Square,
                reads=(xb.b,), writes=(sq.b,))
        rr = AB.alloc(2 * T)
        SM = AFr.alloc(1536)
        for tg in range(TG):
            pt1, pb1 = pp.nxt()
            pt2, pb2 = pp.nxt()
            for c in range(DC):
                P.mm(pt1[0:1, :], self.ocb.a(0, 1),
                     xb.a(c * T + tg * 512, c * T + tg * 512 + 512),
                     start=(c == 0), stop=(c == DC - 1),
                     reads=(self.ocb.b, xb.b), writes=(pb1,))
            for c in range(DC):
                P.mm(pt2[0:1, :], self.ocb.a(0, 1),
                     sq.a(c * T + tg * 512, c * T + tg * 512 + 512),
                     start=(c == 0), stop=(c == DC - 1),
                     reads=(self.ocb.b, sq.b), writes=(pb2,))
            P.v("tensor_scalar_mul", SM.a(0, 512, 0, 1),
                pt1[0:1, :], 1.0 / D, reads=(pb1,), writes=(SM.b,))
            P.v("tensor_scalar_mul", SM.a(512, 1024, 0, 1),
                pt2[0:1, :], 1.0 / D, reads=(pb2,), writes=(SM.b,))
            P.v("scalar_tensor_tensor", SM.a(1024, 1536, 0, 1),
                SM.a(0, 512, 0, 1), -1.0, SM.a(0, 512, 0, 1),
                ALU.mult, ALU.mult, reads=(), writes=(SM.b,))
            P.v("tensor_add", SM.a(512, 1024, 0, 1), SM.a(512, 1024, 0, 1),
                SM.a(1024, 1536, 0, 1), reads=(), writes=(SM.b,))
            P.s("activation", SM.a(512, 1024, 0, 1), SM.a(512, 1024, 0, 1),
                AF.Sqrt, bias=self.sc(1e-5, 0, 1), reads=(self.SC.b,),
                writes=(SM.b,))
            P.v("reciprocal", SM.a(512, 1024, 0, 1), SM.a(512, 1024, 0, 1),
                reads=(), writes=(SM.b,))
            P.v("scalar_tensor_tensor", SM.a(0, 512, 0, 1),
                SM.a(0, 512, 0, 1), -1.0, SM.a(512, 1024, 0, 1),
                ALU.mult, ALU.mult, reads=(), writes=(SM.b,))
            self.replicate_row(SM.a(512, 1024, 0, 1), (SM.b,), 512, rr,
                               tg * 512)
            self.replicate_row(SM.a(0, 512, 0, 1), (SM.b,), 512, rr,
                               T + tg * 512)
        sq.free()
        SM.free()
        for c in range(DC):
            P.v("tensor_mul", xb.a(c * T, (c + 1) * T), src_fn(c),
                rr.a(0, T), reads=tuple(src_bufs) + (rr.b,), writes=(xb.b,))
            P.v("tensor_add", xb.a(c * T, (c + 1) * T),
                xb.a(c * T, (c + 1) * T), rr.a(T, 2 * T),
                reads=(rr.b,), writes=(xb.b,))
        rr.free()
        for c in range(DC):
            if gcol is not None:
                gt, gi = gcol
                bt, bi = bcol
                P.v("tensor_scalar", dst_fn(c), xb.a(c * T, (c + 1) * T),
                    gt.a(gi + c, gi + c + 1), bt.a(bi + c, bi + c + 1),
                    ALU.mult, ALU.add,
                    reads=(xb.b, gt.b, bt.b) + tuple(pbufs),
                    writes=tuple(dst_bufs))
            else:
                P.v("tensor_copy", dst_fn(c), xb.a(c * T, (c + 1) * T),
                    reads=(xb.b,) + tuple(pbufs), writes=tuple(dst_bufs))
        xb.free()

    def pmax_scalar(self, col_ap, col_bufs, dst_ap, dst_bufs, nrow=128):
        P = self.P
        if self._pmax_dram is None:
            self._pmax_dram = self.nc.dram_tensor(
                "pmx_rt", [8, 128], F32, kind="Internal").ap()
            self._pmax_buf = Buf("pmxrt")
        i = self._pmax_n % 8
        self._pmax_n += 1
        row = self.AF.alloc(128)
        P.dma("Q", self._pmax_dram[i, 0:nrow].unsqueeze(1), col_ap,
              reads=tuple(col_bufs), writes=(self._pmax_buf,))
        P.dma("Q", row.a(0, nrow, 0, 1), self._pmax_dram[i:i + 1, 0:nrow],
              reads=(self._pmax_buf,), writes=(row.b,))
        P.v("tensor_reduce", dst_ap, row.a(0, nrow, 0, 1), axis=AX.X,
            op=ALU.max, reads=(row.b,), writes=tuple(dst_bufs))
        row.free()
        return row

    def newton_div(self, dst_ap, num_ap, denom_const, bufs, dst_bufs, tmp):
        P = self.P
        c = float(1.0 / denom_const)
        P.v("tensor_scalar_mul", dst_ap, num_ap, c,
            reads=tuple(bufs), writes=tuple(dst_bufs))
        P.sep(*dst_bufs)
        P.v("scalar_tensor_tensor", tmp, dst_ap, -float(denom_const), num_ap,
            ALU.mult, ALU.add, reads=tuple(bufs) + tuple(dst_bufs),
            writes=tuple(dst_bufs))
        P.sep(*dst_bufs)
        P.v("scalar_tensor_tensor", dst_ap, tmp, c, dst_ap,
            ALU.mult, ALU.add, reads=(), writes=tuple(dst_bufs))
        P.sep(*dst_bufs)

    def recip_refined(self, dst_ap, src_ap, bufs, dst_bufs, tmp):
        P = self.P
        P.v("reciprocal", dst_ap, src_ap, reads=tuple(bufs),
            writes=tuple(dst_bufs))
        P.sep(*dst_bufs)
        P.v("tensor_tensor", tmp, src_ap, dst_ap, ALU.mult,
            reads=tuple(bufs) + tuple(dst_bufs), writes=tuple(dst_bufs))
        P.sep(*dst_bufs)
        P.v("tensor_scalar", tmp, tmp, -1.0, 2.0, ALU.mult, ALU.add,
            reads=(), writes=tuple(dst_bufs))
        P.sep(*dst_bufs)
        P.v("tensor_tensor", dst_ap, dst_ap, tmp, ALU.mult,
            reads=(), writes=tuple(dst_bufs))
        P.sep(*dst_bufs)


def qdq_tile(env, dst_ap, src_ap, rs_col, s_col, lo, hi, sbufs, dbufs,
             colbufs, S1, S2, S3, width):
    P = env.P
    t = S1.a(0, width)
    k = S2.a(0, width)
    d = S3.a(0, width)
    it = env.IT[:, 0:width]
    P.s("activation", t, src_ap, AF.Copy, scale=rs_col,
        reads=tuple(sbufs) + tuple(colbufs), writes=(S1.b,))
    P.v("tensor_copy", it, t, reads=(S1.b,), writes=(env.ITb,))
    P.v("tensor_copy", k, it, reads=(env.ITb,), writes=(S2.b,))
    P.v("tensor_tensor", d, t, k, ALU.subtract,
        reads=(S1.b, S2.b), writes=(S3.b,))
    P.v("tensor_single_scalar", t, d, 0.5, ALU.is_gt,
        reads=(S3.b,), writes=(S1.b,))
    P.g("tensor_tensor", k, k, t, ALU.add, reads=(S1.b,), writes=(S2.b,))
    P.v("tensor_single_scalar", t, d, -0.5, ALU.is_lt,
        reads=(S3.b,), writes=(S1.b,))
    P.g("tensor_tensor", k, k, t, ALU.subtract,
        reads=(S1.b,), writes=(S2.b,))
    P.v("tensor_scalar_min", k, k, float(hi), reads=(), writes=(S2.b,))
    P.v("tensor_scalar_max", k, k, float(lo), reads=(), writes=(S2.b,))
    P.v("tensor_scalar_mul", dst_ap, k, s_col,
        reads=(S2.b,) + tuple(colbufs), writes=tuple(dbufs))


def make_scales(env, gmx_in, K, qmax):
    P, AFr = env.P, env.AF
    g = AFr.alloc(32)
    sc_ = AFr.alloc(16)
    out = AFr.alloc(2 * K)
    P.dma("Q", g.a(0, K * 8, 0, 1), gmx_in, writes=(g.b,))
    P.v("tensor_reduce", g.a(16, 16 + K, 0, 1),
        g.a(0, K * 8, 0, 1).rearrange("p (k c) -> p k c", c=8), axis=AX.X,
        op=ALU.max, reads=(g.b,), writes=(g.b,))
    P.sep(g.b)
    env.newton_div(sc_.a(0, K, 0, 1), g.a(16, 16 + K, 0, 1), qmax,
                   (g.b,), (sc_.b,), g.a(24, 24 + K, 0, 1))
    P.v("tensor_scalar_max", sc_.a(0, K, 0, 1), sc_.a(0, K, 0, 1), 1e-12,
        reads=(), writes=(sc_.b,))
    P.sep(sc_.b)
    env.recip_refined(sc_.a(8, 8 + K, 0, 1), sc_.a(0, K, 0, 1),
                      (sc_.b,), (sc_.b,), g.a(24, 24 + K, 0, 1))
    for k in range(K):
        env.replicate_row(sc_.a(k, k + 1, 0, 1), (sc_.b,), 1, out, 2 * k)
        env.replicate_row(sc_.a(8 + k, 9 + k, 0, 1), (sc_.b,), 1, out,
                          2 * k + 1)
    g.free()
    sc_.free()
    return out



def xres_in(env, padded=False):
    x_h = env.uin("xres", [128, DC * T])
    if not padded:
        t = env.AF.alloc(DC * T)
        env.P.dma("Q", t.a(), x_h, writes=(t.b,))
        return t, T, 0
    stride = T + 4
    t = env.AF.alloc(DC * stride)
    for c in range(DC):
        env.P.v("memset", t.a(c * stride, c * stride + 4), 0.0,
                writes=(t.b,))
        env.P.dma("Q", t.a(c * stride + 4, c * stride + 4 + T),
                  x_h[:, c * T:(c + 1) * T], writes=(t.b,))
    return t, stride, 4


def stream_add_spill(env, mt, tg, psum_ap, psum_buf, xres_h, xout_h, xb,
                     scr, bias_col=None, bias_bufs=()):
    P = env.P
    c0 = mt * T + tg * 512
    P.dma("Q", scr.a(0, 512), xres_h[:, c0:c0 + 512], writes=(scr.b,))
    if bias_col is not None:
        P.v("scalar_tensor_tensor", scr.a(0, 512), psum_ap, bias_col,
            scr.a(0, 512), ALU.add, ALU.add,
            reads=(psum_buf,) + tuple(bias_bufs), writes=(scr.b,))
    else:
        P.v("tensor_add", scr.a(0, 512), scr.a(0, 512), psum_ap,
            reads=(psum_buf,), writes=(scr.b,))
    env.store_ops.append(
        P.dma("Q", xout_h[:, c0:c0 + 512], scr.a(0, 512), reads=(scr.b,)))


def write_y_from_tile(env, xt, y_h):
    P, pp = env.P, env.pp
    ytm = [env.AF.alloc(1024) for _ in range(2)]
    for tt in range(TT):
        st = ytm[tt % 2]
        for c in range(DC):
            pt, pb = pp.nxt()
            P.tr(pt[:, 0:128], xt.a(c * T + tt * 128, c * T + tt * 128 + 128),
                 env.ident.a(), reads=(xt.b, env.ident.b), writes=(pb,))
            P.v("tensor_copy", st.a(c * 128, (c + 1) * 128), pt[:, 0:128],
                reads=(pb,), writes=(st.b,))
        env.store_ops.append(
            P.dma("Q", y_h[tt * 128:(tt + 1) * 128, :], st.a(),
                  reads=(st.b,)))
    for st in ytm:
        st.free()



def prog_A(nc):
    env = Env(nc)
    P, pp, AB, AFr = env.P, env.pp, env.AB, env.AF
    H = 16
    x_in = env.uin("L0_attn_a_x", [T, D])
    w_qkv = env.uin("L0_attn_a_w_qkv", [3 * D, D])
    w_out = env.uin("L0_attn_a_w_out", [D, D])
    b_out = env.uin("L0_attn_a_b_out", [D])
    norm_g = env.uin("L0_attn_a_norm_g", [1, 16, 1, 1])
    norm_b = env.uin("L0_attn_a_norm_b", [1, 16, 1, 1])
    xres_h = env.uout("xres_o", [128, DC * T])

    xres = AFr.alloc(DC * T)

    def xr(c, t0=0, t1=T, p0=0, p1=128):
        return xres.a(c * T + t0, c * T + t1, p0, p1)

    xtm = [AFr.alloc(1024) for _ in range(2)]
    for tt in range(TT):
        st = xtm[tt % 2]
        P.dma("Q", st.a(), x_in[tt * 128:(tt + 1) * 128, :], writes=(st.b,))
        for c in range(DC):
            pt, pb = pp.nxt()
            P.tr(pt[:, 0:128], st.a(c * 128, (c + 1) * 128), env.ident.a(),
                 reads=(st.b, env.ident.b), writes=(pb,))
            P.v("tensor_copy", xr(c, tt * 128, (tt + 1) * 128), pt[:, 0:128],
                reads=(pb,), writes=(xres.b,))
    for st in xtm:
        st.free()

    xn = AB.alloc(DC * T)
    env.layernorm(lambda c: xr(c), (xres.b,),
                  lambda c: xn.a(c * T, (c + 1) * T), (xn.b,))

    NG = AFr.alloc(16)
    NB = AFr.alloc(32)
    BO = AFr.alloc(DC)
    P.dma("Q", NG.a(0, 16, 0, 1), norm_g.rearrange("a h b c -> a (h b c)"),
          writes=(NG.b,))
    P.dma("Q", NB.a(0, 16, 0, 1), norm_b.rearrange("a h b c -> a (h b c)"),
          writes=(NB.b,))
    P.dma("Q", BO.a(0, DC), b_out.rearrange("(c p) -> p c", p=128),
          writes=(BO.b,), slow=True)

    WTkv = env.transpose_w(w_qkv[D:3 * D, :], D, 2 * D)
    gpins = pp.pin(6)
    ktv = [AB.alloc(16 * 130) for _ in range(2)]
    for kt in ktv:
        P.v("memset",
            kt.a().rearrange("p (h s) -> p h s", s=130)[:, :, 64:65], 1.0,
            writes=(kt.b,))
    for tt in range(TT):
        kt = ktv[tt % 2]
        for og in range(4):
            o0 = og * 512
            pt, pb = pp.nxt()
            for kc in range(DC):
                P.mm(pt[:], xn.a(kc * T + tt * 128, kc * T + tt * 128 + 128),
                     WTkv.a(kc * 2 * D + o0, kc * 2 * D + o0 + 512),
                     start=(kc == 0), stop=(kc == DC - 1),
                     reads=(xn.b, WTkv.b), writes=(pb,))
            isv = og >= 2
            h0 = (og % 2) * 8
            dst = kt.a().rearrange("p (h s) -> p h s", s=130)[
                :, h0:h0 + 8, (65 if isv else 0):(129 if isv else 64)]
            P.v("tensor_copy", dst,
                pt[:].rearrange("p (h e) -> p h e", e=64),
                reads=(pb,), writes=(kt.b,))
        for h in range(H):
            j, gt, gb = gpins[h // 3]
            P.mm(gt[0:65, 129 * (h % 3):129 * (h % 3) + 129],
                 kt.a(130 * h, 130 * h + 65),
                 kt.a(130 * h, 130 * h + 129),
                 start=(tt == 0 and h % 3 == 0), stop=(tt == TT - 1),
                 reads=(kt.b,), writes=(gb,))
    for kt in ktv:
        kt.free()
    Gb = AB.alloc(6 * 400)
    SVB = AB.alloc(18 * 64)
    for i in range(6):
        j, gt, gb = gpins[i]
        P.v("tensor_copy", Gb.a(i * 400, i * 400 + 387, 0, 65),
            gt[0:65, 0:387], reads=(gb,), writes=(Gb.b,))
    pp.unpin(gpins)
    WTkv.free()

    def gbase(h):
        return (h // 3) * 400 + (h % 3) * 129

    for g6 in range(6):
        P.dma("Q",
              SVB.a(g6 * 192, g6 * 192 + 192, 0, 1).rearrange(
                  "a (s e) -> a s e", e=64),
              Gb.a(g6 * 400, g6 * 400 + 387, 64, 65).rearrange(
                  "a (s r) -> a s r", r=129)[:, :, 65:129],
              reads=(Gb.b,), writes=(SVB.b,))
    P.dma("Q", Gb.a(0, 6 * 400, 64, 128), Gb.a(0, 6 * 400, 0, 64),
          reads=(Gb.b,), writes=(Gb.b,))
    P.v("tensor_mul",
        SVB.a(0, 18 * 64, 0, 1).rearrange("a (h e) -> a h e", e=64),
        SVB.a(0, 18 * 64, 0, 1).rearrange("a (h e) -> a h e", e=64),
        NB.a(0, 18, 0, 1).unsqueeze(2).broadcast_to([1, 18, 64]),
        reads=(NB.b,), writes=(SVB.b,))

    QB = AB.alloc(DC * T)
    WTq = env.transpose_w(w_qkv[0:D, :], D, D, piece=512)
    for mt in range(DC):
        for tg in range(TG):
            pt, pb = pp.nxt()
            for kc in range(DC):
                P.mm(pt[:], WTq.a(kc * D + mt * 128, kc * D + mt * 128 + 128),
                     xn.a(kc * T + tg * 512, kc * T + tg * 512 + 512),
                     start=(kc == 0), stop=(kc == DC - 1),
                     reads=(WTq.b, xn.b), writes=(pb,))
            P.s("copy", QB.a(mt * T + tg * 512, mt * T + tg * 512 + 512),
                pt[:], reads=(pb,), writes=(QB.b,))
    WTq.free()
    xn.free()
    atto = AB.alloc(DC * T)
    QMQ = AFr.alloc(512)
    SSS = AFr.alloc(3 * 512)
    QN = AB.alloc(512)
    for h in range(H):
        hb = gbase(h)
        p0 = 64 * (h % 2)
        p1 = p0 + 64
        qc = h // 2
        for tg in range(TG):
            t0 = tg * 512
            ptA, pbA = pp.nxt()
            P.mm(ptA[p0:p1, :], Gb.a(hb, hb + 64, p0, p1),
                 QB.a(qc * T + t0, qc * T + t0 + 512, p0, p1),
                 start=True, stop=True, reads=(Gb.b, QB.b), writes=(pbA,),
                 tp=(p0, p0))
            P.v("tensor_mul", QMQ.a(0, 512, p0, p1),
                QB.a(qc * T + t0, qc * T + t0 + 512, p0, p1),
                ptA[p0:p1, :], reads=(QB.b, pbA), writes=(QMQ.b,))
            ptB, pbB = pp.nxt()
            P.mm(ptB[0:1, :], env.ocf.a(0, 1, p0, p1), QMQ.a(0, 512, p0, p1),
                 start=True, stop=True, reads=(env.ocf.b, QMQ.b),
                 writes=(pbB,), tp=(p0, 0))
            P.v("tensor_scalar_max", SSS.a(0, 512, 0, 1), ptB[0:1, :],
                1e-24, reads=(pbB,), writes=(SSS.b,))
            P.s("activation", SSS.a(512, 1024, 0, 1), SSS.a(0, 512, 0, 1),
                AF.Sqrt, reads=(SSS.b,), writes=(SSS.b,))
            P.v("reciprocal", SSS.a(1024, 1536, 0, 1), SSS.a(512, 1024, 0, 1),
                reads=(SSS.b,), writes=(SSS.b,))
            P.v("tensor_scalar_mul", SSS.a(1024, 1536, 0, 1),
                SSS.a(1024, 1536, 0, 1), NG.a(h, h + 1, 0, 1),
                reads=(NG.b,), writes=(SSS.b,))
            ptC, pbC = pp.nxt()
            P.mm(ptC[p0:p1, :], env.orow.a(0, 64, 0, 1),
                 SSS.a(1024, 1536, 0, 1),
                 start=True, stop=True, reads=(env.orow.b, SSS.b),
                 writes=(pbC,), tp=(0, p0))
            P.v("tensor_mul", QN.a(0, 512, p0, p1),
                QB.a(qc * T + t0, qc * T + t0 + 512, p0, p1),
                ptC[p0:p1, :], reads=(QB.b, pbC), writes=(QN.b,))
            ptD, pbD = pp.nxt()
            P.mm(ptD[p0:p1, :], Gb.a(hb + 65, hb + 129, p0, p1),
                 QN.a(0, 512, p0, p1),
                 start=True, stop=False, reads=(Gb.b, QN.b), writes=(pbD,),
                 tp=(p0, p0))
            P.mm(ptD[p0:p1, :],
                 SVB.a((h // 3) * 192 + (h % 3) * 64,
                       (h // 3) * 192 + (h % 3) * 64 + 64, 0, 1),
                 env.orb.a(0, 128, 0, 1).unsqueeze(1).broadcast_to([1, 4, 128]),
                 start=False, stop=True, reads=(SVB.b, env.orb.b),
                 writes=(pbD,), tp=(0, p0))
            P.v("tensor_copy",
                atto.a(qc * T + t0, qc * T + t0 + 512, p0, p1),
                ptD[p0:p1, :], reads=(pbD,), writes=(atto.b,))
    QB.free()
    Gb.free()
    SVB.free()
    QMQ.free()
    SSS.free()
    QN.free()

    WoT = env.transpose_w(w_out, D, D)
    for mt in range(DC):
        for tg in range(TG):
            pt, pb = pp.nxt()
            for kc in range(DC):
                P.mm(pt[:], WoT.a(kc * D + mt * 128, kc * D + mt * 128 + 128),
                     atto.a(kc * T + tg * 512, kc * T + tg * 512 + 512),
                     reads=(WoT.b, atto.b), writes=(pb,),
                     start=(kc == 0), stop=(kc == DC - 1))
            P.v("scalar_tensor_tensor",
                xr(mt, tg * 512, tg * 512 + 512), pt[:],
                BO.a(mt, mt + 1), xr(mt, tg * 512, tg * 512 + 512),
                ALU.add, ALU.add,
                reads=(pb, BO.b, xres.b), writes=(xres.b,))
    atto.free()
    WoT.free()
    env.store_ops.append(P.dma("Q", xres_h, xres.a(), reads=(xres.b,)))
    env.finish()
    return env



def prog_E(nc):
    env = Env(nc)
    P, AFr = env.P, env.AF
    xt, stride, off = xres_in(env)
    xn_h = env.uout("xnq", [128, DC * T])
    mx_h = env.uout("mx", [1, 1])
    env.layernorm(lambda c: xt.a(c * T, (c + 1) * T), (xt.b,),
                  lambda c: xt.a(c * T, (c + 1) * T), (xt.b,), dst_dt="f32")
    mcol = AFr.alloc(16)
    for c in range(DC):
        P.v("tensor_reduce", mcol.a(c, c + 1), xt.a(c * T, (c + 1) * T),
            axis=AX.X, op=ALU.max, apply_absolute_value=True,
            reads=(xt.b,), writes=(mcol.b,))
    P.sep(mcol.b)
    P.v("tensor_reduce", mcol.a(8, 9), mcol.a(0, 8), axis=AX.X, op=ALU.max,
        reads=(mcol.b,), writes=(mcol.b,))
    P.sep(mcol.b)
    mx1 = AFr.alloc(16)
    env.pmax_scalar(mcol.a(8, 9), (mcol.b,), mx1.a(0, 1, 0, 1), (mx1.b,))
    env.store_ops.append(P.dma("Q", xn_h, xt.a(), reads=(xt.b,)))
    env.store_ops.append(P.dma("Q", mx_h, mx1.a(0, 1, 0, 1),
                               reads=(mx1.b,)))
    env.finish()
    return env



def prog_P(nc):
    env = Env(nc)
    P, AFr = env.P, env.AF
    xres_h = env.uin("xres", [128, DC * T])
    xloc_h = env.uin("xloc", [128, DC * T])
    xout_h = env.uout("xres_o", [128, DC * T])
    a = AFr.alloc(2048)
    b = AFr.alloc(2048)
    for c in range(DC):
        P.dma("Q", a.a(), xres_h[:, c * T:(c + 1) * T], writes=(a.b,))
        P.dma("Q", b.a(), xloc_h[:, c * T:(c + 1) * T], writes=(b.b,))
        P.v("tensor_add", a.a(), a.a(), b.a(), reads=(b.b,), writes=(a.b,))
        env.store_ops.append(
            P.dma("Q", xout_h[:, c * T:(c + 1) * T], a.a(), reads=(a.b,)))
    env.finish()
    return env



def prog_O(nc):
    env = Env(nc)
    xt, _, _ = xres_in(env)
    y_h = env.uout("y", [T, D])
    write_y_from_tile(env, xt, y_h)
    env.finish()
    return env



def prog_mamba(nc, v1):
    def bail(env, xout_h):
        zt = env.AF.alloc(2048)
        env.P.v("memset", zt.a(), 0.0, writes=(zt.b,))
        for c in range(DC):
            env.store_ops.append(
                env.P.dma("Q", xout_h[:, c * T:(c + 1) * T], zt.a(),
                          reads=(zt.b,)))
        env.finish()
        return env
    env = Env(nc)
    P, pp, AB, AFr = env.P, env.pp, env.AB, env.AF
    E = 2048
    EC = E // 128
    NS = 16
    DTR = 64
    pre = "L1_attn_a_" if v1 else "L0_attn_b_"
    if v1:
        w_in = env.uin(pre + "W_in", [2 * E, D])
        conv_w = env.uin(pre + "conv_w", [E, 4])
        conv_b = env.uin(pre + "conv_b", [E])
        w_xp = env.uin(pre + "W_xp", [DTR + 2 * NS, E])
        w_dt = env.uin(pre + "W_dt", [E, DTR])
        b_dt = env.uin(pre + "b_dt", [E])
        a_log = env.uin(pre + "A_log", [E, NS])
        d_skip = env.uin(pre + "D", [E])
        w_out = env.uin(pre + "W_out", [D, E])
    else:
        w_in = env.uin(pre + "in_proj_w", [D, 2 * E])
        conv_w = env.uin(pre + "conv_w", [E, 1, 4])
        conv_b = env.uin(pre + "conv_b", [E])
        w_xp = env.uin(pre + "x_proj_w", [E, DTR + 2 * NS])
        w_dt = env.uin(pre + "dt_proj_w", [DTR, E])
        b_dt = env.uin(pre + "dt_proj_b", [E])
        a_log = env.uin(pre + "A_log", [E, NS])
        d_skip = env.uin(pre + "D_skip", [E])
        w_out = env.uin(pre + "out_proj_w", [E, D])
    xres_h = env.uin("xres", [128, DC * T])
    xout_h = env.uout("xres_o", [128, DC * T])
    z_h = nc.dram_tensor("ztmp", [128, EC * T], BF16, kind="Internal").ap()
    zb_ = Buf("ztmp")

    xt = AFr.alloc(DC * T)
    P.dma("Q", xt.a(), xres_h, writes=(xt.b,))
    xn = AB.alloc(DC * T)
    env.layernorm(lambda c: xt.a(c * T, (c + 1) * T), (xt.b,),
                  lambda c: xn.a(c * T, (c + 1) * T), (xn.b,))
    xt.free()

    CW = AFr.alloc(4 * EC)
    CB = AFr.alloc(EC)
    DB = AFr.alloc(EC)
    DS = AFr.alloc(EC)
    cwv = conv_w.rearrange("e a k -> e (a k)") if not v1 else conv_w
    for c in range(EC):
        P.dma("Q", CW.a(4 * c, 4 * c + 4), cwv[c * 128:(c + 1) * 128, :],
              writes=(CW.b,))
    P.dma("Q", CB.a(0, EC), conv_b.rearrange("(c p) -> p c", p=128),
          writes=(CB.b,), slow=True)
    P.dma("Q", DB.a(0, EC), b_dt.rearrange("(c p) -> p c", p=128),
          writes=(DB.b,), slow=True)
    P.dma("Q", DS.a(0, EC), d_skip.rearrange("(c p) -> p c", p=128),
          writes=(DS.b,), slow=True)
    AT = AFr.alloc(EC * NS)
    P.dma("Q", AT.a().rearrange("p (c n) -> p c n", c=EC),
          a_log.rearrange("(c p) n -> p c n", p=128), writes=(AT.b,))
    P.s("activation", AT.a(), AT.a(), AF.Exp, reads=(), writes=(AT.b,))
    P.v("tensor_scalar_mul", AT.a(), AT.a(), -1.0, reads=(), writes=(AT.b,))

    stride = T + 4
    xslot = AB.alloc(EC * stride)
    for c in range(EC):
        P.v("memset", xslot.a(c * stride, c * stride + 4), 0.0,
            writes=(xslot.b,))
    zstg = AB.alloc(512)
    pieces = [AB.alloc(DC * 128) for _ in range(2)]
    for ec in range(2 * EC):
        pc = pieces[ec % 2]
        if v1:
            raw = AB.alloc(DC * 128)
            P.dma("P", raw.a(),
                  w_in[ec * 128:(ec + 1) * 128, :], reads=(pc.b,),
                  writes=(raw.b,))
            for kc in range(DC):
                P.dmat("Q", pc.a(kc * 128, (kc + 1) * 128),
                       raw.a(kc * 128, (kc + 1) * 128),
                       reads=(raw.b,), writes=(pc.b,))
            raw.free()
        else:
            for kc in range(DC):
                P.dma("P", pc.a(kc * 128, (kc + 1) * 128),
                      w_in[kc * 128:(kc + 1) * 128,
                           ec * 128:(ec + 1) * 128], writes=(pc.b,))
        for tg in range(TG):
            pt, pb = pp.nxt()
            for kc in range(DC):
                P.mm(pt[:], pc.a(kc * 128, (kc + 1) * 128),
                     xn.a(kc * T + tg * 512, kc * T + tg * 512 + 512),
                     start=(kc == 0), stop=(kc == DC - 1),
                     reads=(pc.b, xn.b), writes=(pb,))
            if ec < EC:
                P.v("tensor_copy",
                    xslot.a(ec * stride + 4 + tg * 512,
                            ec * stride + 4 + tg * 512 + 512),
                    pt[:], reads=(pb,), writes=(xslot.b,))
            else:
                c = ec - EC
                P.v("tensor_copy", zstg.a(0, 512), pt[:], reads=(pb,),
                    writes=(zstg.b,))
                P.dma("Q", z_h[:, c * T + tg * 512:c * T + tg * 512 + 512],
                      zstg.a(0, 512), reads=(zstg.b,), writes=(zb_,))
    for pc_ in pieces:
        pc_.free()
    zstg.free()
    xn.free()
    if MCUT <= 2:
        return bail(env, xout_h)

    acc = AFr.alloc(T)
    for c in range(EC):
        base = c * stride
        P.v("tensor_scalar_mul", acc.a(0, T), xslot.a(base + 1, base + 1 + T),
            CW.a(4 * c, 4 * c + 1), reads=(xslot.b, CW.b), writes=(acc.b,))
        for k in (1, 2):
            P.v("scalar_tensor_tensor", acc.a(0, T),
                xslot.a(base + 1 + k, base + 1 + k + T),
                CW.a(4 * c + k, 4 * c + k + 1), acc.a(0, T),
                ALU.mult, ALU.add, reads=(xslot.b, CW.b), writes=(acc.b,))
        P.v("scalar_tensor_tensor", acc.a(0, T),
            xslot.a(base + 4, base + 4 + T),
            CW.a(4 * c + 3, 4 * c + 4), acc.a(0, T),
            ALU.mult, ALU.add, reads=(xslot.b, CW.b), writes=(acc.b,))
        P.s("activation", xslot.a(base + 4, base + 4 + T), acc.a(0, T),
            AF.Silu, bias=CB.a(c, c + 1),
            reads=(acc.b, CB.b), writes=(xslot.b,))
    acc.free()
    if MCUT <= 3:
        return bail(env, xout_h)

    def xc(c, t0, t1):
        return xslot.a(c * stride + 4 + t0, c * stride + 4 + t1)

    NXP = DTR + 2 * NS
    if v1:
        raw = AB.alloc(T)
        P.v("memset", raw.a(0, T, 96, 128), 0.0, writes=(raw.b,))
        P.dma("P", raw.a(0, T, 0, 96), w_xp, writes=(raw.b,))
        XPW = AB.alloc(EC * 128)
        for ec in range(EC):
            P.dmat("Q", XPW.a(ec * 128, (ec + 1) * 128),
                   raw.a(ec * 128, (ec + 1) * 128),
                   reads=(raw.b,), writes=(XPW.b,))
        raw.free()

        def xpw_ap(ec):
            return XPW.a(ec * 128, ec * 128 + NXP)
    else:
        XPW = env.load_w_bf16(w_xp, E, NXP)

        def xpw_ap(ec):
            return XPW.a(ec * NXP, (ec + 1) * NXP)

    XDB = AFr.alloc(T)
    for tg in range(TG):
        pt, pb = pp.nxt()
        for ec in range(EC):
            P.mm(pt[0:NXP, :], xpw_ap(ec), xc(ec, tg * 512, tg * 512 + 512),
                 start=(ec == 0), stop=(ec == EC - 1),
                 reads=(XPW.b, xslot.b), writes=(pb,))
        P.v("tensor_copy", XDB.a(tg * 512, (tg + 1) * 512, 0, NXP),
            pt[0:NXP, :], reads=(pb,), writes=(XDB.b,))
    XPW.free()
    XDBB = AB.alloc(T)
    P.v("tensor_copy", XDBB.a(0, T, 0, DTR), XDB.a(0, T, 0, DTR),
        reads=(XDB.b,), writes=(XDBB.b,))
    if MCUT <= 4:
        return bail(env, xout_h)

    TH = T // 4
    delta = AB.alloc(EC * TH)
    du = AB.alloc(EC * TH)
    YB = delta

    if v1:
        DTW = AB.alloc(EC * 128)
        raw = AB.alloc(128)
        for ec in range(EC):
            P.v("memset", raw.a(64, 128), 0.0, writes=(raw.b,))
            P.dma("P", raw.a(0, 64),
                  w_dt[ec * 128:(ec + 1) * 128, :], writes=(raw.b,))
            P.dmat("Q", DTW.a(ec * 128, (ec + 1) * 128), raw.a(0, 128),
                   reads=(raw.b,), writes=(DTW.b,))
        raw.free()

        def dtw_ap(ec):
            return DTW.a(ec * 128, (ec + 1) * 128, 0, 64)
    else:
        DTW = AB.alloc(EC * 128)
        P.dma("P", DTW.a(0, EC * 128, 0, 64), w_dt, writes=(DTW.b,))

        def dtw_ap(ec):
            return DTW.a(ec * 128, (ec + 1) * 128, 0, 64)

    spl = AFr.alloc(512)
    spm = AFr.alloc(512)
    ZER = AFr.alloc(512)
    P.v("memset", ZER.a(), 0.0, writes=(ZER.b,))
    STATE = AFr.alloc(EC * NS)
    P.v("memset", STATE.a(), 0.0, writes=(STATE.b,))
    BC = AFr.alloc(2 * TH)
    RWm = AFr.alloc(2 * TH)
    dA = AFr.alloc(TH)
    dBu = AFr.alloc(TH)
    SS = AFr.alloc(TH)
    YAC = AFr.alloc(EC * TH)
    zs = AB.alloc(TH)
    zf = AFr.alloc(TH)
    scr = AFr.alloc(512)
    wps = [AB.alloc(EC * 128) for _ in range(2)]
    if MCUT <= 4.2:
        return bail(env, xout_h)
    for th in range(4):
        t0 = th * TH
        for ec in range(EC):
            pt, pb = pp.nxt()
            P.mm(pt[:], dtw_ap(ec), XDBB.a(t0, t0 + TH, 0, 64),
                 start=True, stop=True, reads=(DTW.b, XDBB.b), writes=(pb,))
            P.s("activation", spl.a(0, TH), pt[:], AF.Abs,
                bias=DB.a(ec, ec + 1), reads=(pb, DB.b), writes=(spl.b,))
            P.s("activation", spl.a(0, TH), spl.a(0, TH), AF.Exp,
                scale=env.sc(-1.0), reads=(env.SC.b,), writes=(spl.b,))
            P.s("activation", spl.a(0, TH), spl.a(0, TH), AF.Ln,
                bias=env.sc(1.0), reads=(env.SC.b,), writes=(spl.b,))
            P.s("activation", spm.a(0, TH), pt[:], AF.Relu,
                bias=DB.a(ec, ec + 1), reads=(pb, DB.b), writes=(spm.b,))
            P.v("tensor_add", delta.a(ec * TH, (ec + 1) * TH),
                spm.a(0, TH), spl.a(0, TH),
                reads=(spm.b, spl.b), writes=(delta.b,))
            P.v("tensor_mul", du.a(ec * TH, (ec + 1) * TH),
                delta.a(ec * TH, (ec + 1) * TH), xc(ec, t0, t0 + TH),
                reads=(delta.b, xslot.b), writes=(du.b,))
        P.v("memset", YAC.a(), 0.0, writes=(YAC.b,))
        for n in range(NS if MCUT > 4.6 else 0):
            P.dma("Q", RWm.a(0, TH, 0, 1),
                  XDB.a(t0, t0 + TH, DTR + n, DTR + n + 1),
                  reads=(XDB.b,), writes=(RWm.b,))
            P.dma("Q", RWm.a(TH, 2 * TH, 0, 1),
                  XDB.a(t0, t0 + TH, DTR + NS + n, DTR + NS + n + 1),
                  reads=(XDB.b,), writes=(RWm.b,))
            env.replicate_row(RWm.a(0, TH, 0, 1), (RWm.b,), TH, BC, 0)
            env.replicate_row(RWm.a(TH, 2 * TH, 0, 1), (RWm.b,), TH, BC, TH)
            for ec in range(EC):
                P.s("activation", dA.a(0, TH),
                    delta.a(ec * TH, (ec + 1) * TH), AF.Exp,
                    scale=AT.a(ec * NS + n, ec * NS + n + 1),
                    reads=(delta.b, AT.b), writes=(dA.b,))
                P.g("tensor_mul", dBu.a(0, TH),
                    du.a(ec * TH, (ec + 1) * TH), BC.a(0, TH),
                    reads=(du.b, BC.b), writes=(dBu.b,))
                P.v("tensor_tensor_scan", SS.a(0, TH), dA.a(0, TH),
                    dBu.a(0, TH), STATE.a(ec * NS + n, ec * NS + n + 1),
                    ALU.mult, ALU.add,
                    reads=(dA.b, dBu.b, STATE.b), writes=(SS.b,))
                P.v("tensor_copy", STATE.a(ec * NS + n, ec * NS + n + 1),
                    SS.a(TH - 1, TH), reads=(SS.b,), writes=(STATE.b,))
                P.g("tensor_mul", dBu.a(0, TH), SS.a(0, TH), BC.a(TH, 2 * TH),
                    reads=(SS.b, BC.b), writes=(dBu.b,))
                P.v("tensor_add", YAC.a(ec * TH, (ec + 1) * TH),
                    YAC.a(ec * TH, (ec + 1) * TH), dBu.a(0, TH),
                    reads=(dBu.b,), writes=(YAC.b,))
        for ec in range(EC if MCUT > 4.8 else 0):
            P.v("scalar_tensor_tensor", YAC.a(ec * TH, (ec + 1) * TH),
                xc(ec, t0, t0 + TH), DS.a(ec, ec + 1),
                YAC.a(ec * TH, (ec + 1) * TH), ALU.mult, ALU.add,
                reads=(xslot.b, DS.b), writes=(YAC.b,))
            P.dma("Q", zs.a(0, TH), z_h[:, ec * T + t0:ec * T + t0 + TH],
                  reads=(zb_,), writes=(zs.b,))
            P.s("activation", zf.a(0, TH), zs.a(0, TH), AF.Silu,
                reads=(zs.b,), writes=(zf.b,))
            P.v("tensor_mul", YB.a(ec * TH, (ec + 1) * TH),
                YAC.a(ec * TH, (ec + 1) * TH), zf.a(0, TH),
                reads=(YAC.b, zf.b), writes=(YB.b,))
        for dc in range(DC if MCUT > 5 else 0):
            wp = wps[dc % 2]
            if v1:
                raw2 = AB.alloc(EC * 128)
                P.dma("P", raw2.a(), w_out[dc * 128:(dc + 1) * 128, :],
                      reads=(wp.b,), writes=(raw2.b,))
                for ec in range(EC):
                    P.dmat("Q", wp.a(ec * 128, (ec + 1) * 128),
                           raw2.a(ec * 128, (ec + 1) * 128),
                           reads=(raw2.b,), writes=(wp.b,))
                raw2.free()
            else:
                for ec in range(EC):
                    P.dma("P", wp.a(ec * 128, (ec + 1) * 128),
                          w_out[ec * 128:(ec + 1) * 128,
                                dc * 128:(dc + 1) * 128], writes=(wp.b,))
            pt, pb = pp.nxt()
            for ec in range(EC):
                P.mm(pt[:], wp.a(ec * 128, (ec + 1) * 128),
                     YB.a(ec * TH, (ec + 1) * TH),
                     start=(ec == 0), stop=(ec == EC - 1),
                     reads=(wp.b, YB.b), writes=(pb,))
            stream_add_spill(env, dc, th, pt[:], pb, xres_h, xout_h, None,
                             scr)
    if MCUT <= 5:
        return bail(env, xout_h)
    env.finish()
    return env



def dequant_piece(env, raw_ap, raw_bufs, dst_ap, dst_bufs, width, S1, S2, S3,
                  MB):
    P = env.P
    nb = width // 32
    mab = MB.a(0, nb)
    s_ = MB.a(32, 32 + nb)
    rs = MB.a(64, 64 + nb)
    tmpc = MB.a(96, 96 + nb)
    P.v("tensor_reduce", mab,
        raw_ap.rearrange("p (b e) -> p b e", e=32), axis=AX.X, op=ALU.max,
        apply_absolute_value=True, reads=tuple(raw_bufs), writes=(MB.b,))
    P.sep(MB.b)
    env.newton_div(s_, mab, 7.0, (MB.b,), (MB.b,), tmpc)
    P.v("tensor_scalar_max", s_, s_, 1e-12, reads=(), writes=(MB.b,))
    P.sep(MB.b)
    env.recip_refined(rs, s_, (MB.b,), (MB.b,), tmpc)
    t = S1.a(0, width)
    k = S2.a(0, width)
    d = S3.a(0, width)
    it = env.IT[:, 0:width]
    P.v("tensor_tensor", t.rearrange("p (b e) -> p b e", e=32),
        raw_ap.rearrange("p (b e) -> p b e", e=32),
        rs.rearrange("p (b e) -> p b e", e=1).broadcast_to([128, nb, 32]),
        ALU.mult, reads=tuple(raw_bufs) + (MB.b,), writes=(S1.b,))
    P.v("tensor_copy", it, t, reads=(S1.b,), writes=(env.ITb,))
    P.v("tensor_copy", k, it, reads=(env.ITb,), writes=(S2.b,))
    P.v("tensor_tensor", d, t, k, ALU.subtract,
        reads=(S1.b, S2.b), writes=(S3.b,))
    P.v("tensor_single_scalar", t, d, 0.5, ALU.is_gt,
        reads=(S3.b,), writes=(S1.b,))
    P.g("tensor_tensor", k, k, t, ALU.add, reads=(S1.b,), writes=(S2.b,))
    P.v("tensor_single_scalar", t, d, -0.5, ALU.is_lt,
        reads=(S3.b,), writes=(S1.b,))
    P.g("tensor_tensor", k, k, t, ALU.subtract,
        reads=(S1.b,), writes=(S2.b,))
    P.v("tensor_scalar_min", k, k, 7.0, reads=(), writes=(S2.b,))
    P.v("tensor_scalar_max", k, k, -8.0, reads=(), writes=(S2.b,))
    P.v("tensor_tensor", dst_ap.rearrange("p (b e) -> p b e", e=32),
        k.rearrange("p (b e) -> p b e", e=32),
        s_.rearrange("p (b e) -> p b e", e=1).broadcast_to([128, nb, 32]),
        ALU.mult, reads=(S2.b, MB.b), writes=tuple(dst_bufs))



def prog_F(nc):
    env = Env(nc)
    P, pp, AB, AFr = env.P, env.pp, env.AB, env.AF
    OC = 24
    xn_h = env.uin("xnq", [128, DC * T])
    gmx_h = env.uin("gmx", [1, 8])
    w_up = env.uin("L1_ffn_w_up", [3072, 1024])
    w_gate = env.uin("L1_ffn_w_gate", [3072, 1024])
    up_h = env.uout("upraw", [128, OC * T])
    gt_h = env.uout("gateraw", [128, OC * T])
    mx_h = env.uout("mx", [1, 2])

    S1 = AFr.alloc(1024)
    S2 = AFr.alloc(1024)
    S3 = AFr.alloc(1024)
    MB = AFr.alloc(128)
    scl = make_scales(env, gmx_h, 1, 32767.0)
    xn = AFr.alloc(DC * T)
    P.dma("Q", xn.a(), xn_h, writes=(xn.b,))
    XQ = AB.alloc(DC * T)
    for c in range(DC):
        for half in range(2):
            o = c * T + half * 1024
            qdq_tile(env, XQ.a(o, o + 1024), xn.a(o, o + 1024),
                     scl.a(1, 2), scl.a(0, 1), -32768.0, 32767.0,
                     (xn.b,), (XQ.b,), (scl.b,), S1, S2, S3, 1024)
    xn.free()

    fdbg_h = env.uout("fdbg", [128, 3 * DC * T]) if FDBG else None
    if FDBG:
        env.store_ops.append(
            P.dma("P", fdbg_h[:, 0:DC * T], XQ.a(), reads=(XQ.b,)))
    RM = AFr.alloc(16)
    P.v("memset", RM.a(0, 2), 0.0, writes=(RM.b,))
    RAW = AFr.alloc(1024)
    WQB = AB.alloc(1024)
    WT = [AB.alloc(DC * 128) for _ in range(2)]
    STG = [AFr.alloc(512) for _ in range(2)]
    mxt = AFr.alloc(16)
    for wi, (w_hbm, out_h) in enumerate(((w_up, up_h), (w_gate, gt_h))):
        for oc in range(OC):
            P.dma("Q", RAW.a(), w_hbm[oc * 128:(oc + 1) * 128, :],
                  writes=(RAW.b,))
            dequant_piece(env, RAW.a(), (RAW.b,), WQB.a(), (WQB.b,),
                          1024, S1, S2, S3, MB)
            wt = WT[oc % 2]
            for hc in range(DC):
                P.dmat("Q", wt.a(hc * 128, (hc + 1) * 128),
                       WQB.a(hc * 128, (hc + 1) * 128),
                       reads=(WQB.b,), writes=(wt.b,))
            if FDBG and wi == 0 and oc == 0:
                env.store_ops.append(
                    P.dma("P", fdbg_h[:, DC * T:DC * T + 1024], WQB.a(),
                          reads=(WQB.b,)))
                env.store_ops.append(
                    P.dma("P", fdbg_h[:, DC * T + 1024:DC * T + 2048],
                          wt.a(), reads=(wt.b,)))
            for tg in range(TG):
                pt, pb = pp.nxt()
                for hc in range(DC):
                    P.mm(pt[:], wt.a(hc * 128, (hc + 1) * 128),
                         XQ.a(hc * T + tg * 512, hc * T + tg * 512 + 512),
                         start=(hc == 0), stop=(hc == DC - 1),
                         reads=(wt.b, XQ.b), writes=(pb,))
                st = STG[tg % 2]
                P.v("tensor_copy", st.a(0, 512), pt[:], reads=(pb,),
                    writes=(st.b,))
                P.v("tensor_reduce", mxt.a(0, 1), st.a(0, 512), axis=AX.X,
                    op=ALU.max, apply_absolute_value=True,
                    reads=(st.b,), writes=(mxt.b,))
                P.sep(mxt.b)
                P.v("tensor_tensor", RM.a(wi, wi + 1), RM.a(wi, wi + 1),
                    mxt.a(0, 1), ALU.max, reads=(mxt.b,), writes=(RM.b,))
                P.sep(RM.b)
                env.store_ops.append(
                    P.dma("Q",
                          out_h[:, oc * T + tg * 512:oc * T + tg * 512 + 512],
                          st.a(0, 512), reads=(st.b,)))
    mxo = AFr.alloc(16)
    env.pmax_scalar(RM.a(0, 1), (RM.b,), mxo.a(0, 1, 0, 1), (mxo.b,))
    env.pmax_scalar(RM.a(1, 2), (RM.b,), mxo.a(1, 2, 0, 1), (mxo.b,))
    env.store_ops.append(P.dma("Q", mx_h, mxo.a(0, 2, 0, 1),
                               reads=(mxo.b,)))
    env.finish()
    return env



def prog_G(nc):
    env = Env(nc)
    P, AFr = env.P, env.AF
    OC = 24
    gt_h = env.uin("gateraw", [128, OC * T])
    gmx_h = env.uin("gmx", [1, 8])
    g2_h = env.uout("g2raw", [128, OC * T])
    mx_h = env.uout("mx", [1, 1])

    S1 = AFr.alloc(1024)
    S2 = AFr.alloc(1024)
    S3 = AFr.alloc(1024)
    scl = make_scales(env, gmx_h, 1, 32767.0)
    FC = AFr.alloc(16)
    P.v("tensor_scalar_mul", FC.a(0, 1), env.ocf.a(0, 1), 65536.0,
        reads=(env.ocf.b,), writes=(FC.b,))
    P.v("tensor_scalar_mul", FC.a(1, 2), env.ocf.a(0, 1), 1.0 / 65536.0,
        reads=(env.ocf.b,), writes=(FC.b,))
    RM = AFr.alloc(16)
    P.v("memset", RM.a(0, 1), 0.0, writes=(RM.b,))
    mxt = AFr.alloc(16)
    LD = [AFr.alloc(1024) for _ in range(2)]
    GQ = AFr.alloc(1024)
    SG = AFr.alloc(1024)
    ST = [AFr.alloc(1024) for _ in range(2)]
    for oc in range(OC):
        for half in range(2):
            ld = LD[half]
            o = oc * T + half * 1024
            P.dma("Q", ld.a(), gt_h[:, o:o + 1024], writes=(ld.b,))
            qdq_tile(env, GQ.a(), ld.a(), scl.a(1, 2), scl.a(0, 1),
                     -32768.0, 32767.0, (ld.b,), (GQ.b,), (scl.b,),
                     S1, S2, S3, 1024)
            P.s("activation", SG.a(), GQ.a(), AF.Sigmoid,
                reads=(GQ.b,), writes=(SG.b,))
            qdq_tile(env, SG.a(), SG.a(), FC.a(0, 1), FC.a(1, 2),
                     0.0, 65535.0, (SG.b,), (SG.b,), (FC.b,),
                     S1, S2, S3, 1024)
            st = ST[half]
            P.v("tensor_tensor", st.a(), GQ.a(), SG.a(), ALU.mult,
                reads=(GQ.b, SG.b), writes=(st.b,))
            P.v("tensor_reduce", mxt.a(0, 1), st.a(), axis=AX.X,
                op=ALU.max, apply_absolute_value=True,
                reads=(st.b,), writes=(mxt.b,))
            P.sep(mxt.b)
            P.v("tensor_tensor", RM.a(0, 1), RM.a(0, 1), mxt.a(0, 1),
                ALU.max, reads=(mxt.b,), writes=(RM.b,))
            P.sep(RM.b)
            env.store_ops.append(
                P.dma("Q", g2_h[:, o:o + 1024], st.a(), reads=(st.b,)))
    mxo = AFr.alloc(16)
    env.pmax_scalar(RM.a(0, 1), (RM.b,), mxo.a(0, 1, 0, 1), (mxo.b,))
    env.store_ops.append(P.dma("Q", mx_h, mxo.a(0, 1, 0, 1),
                               reads=(mxo.b,)))
    env.finish()
    return env



def prog_H(nc):
    env = Env(nc)
    P, AFr = env.P, env.AF
    OC = 24
    up_h = env.uin("upraw", [128, OC * T])
    g2_h = env.uin("g2raw", [128, OC * T])
    gmx_h = env.uin("gmx", [1, 16])
    o_h = env.uout("oraw", [128, OC * T])
    mx_h = env.uout("mx", [1, 1])

    S1 = AFr.alloc(1024)
    S2 = AFr.alloc(1024)
    S3 = AFr.alloc(1024)
    scl = make_scales(env, gmx_h, 2, 32767.0)
    RM = AFr.alloc(16)
    P.v("memset", RM.a(0, 1), 0.0, writes=(RM.b,))
    mxt = AFr.alloc(16)
    LU = [AFr.alloc(1024) for _ in range(2)]
    LG = [AFr.alloc(1024) for _ in range(2)]
    UQ = AFr.alloc(1024)
    ST = [AFr.alloc(1024) for _ in range(2)]
    for oc in range(OC):
        for half in range(2):
            o = oc * T + half * 1024
            lu = LU[half]
            lg = LG[half]
            P.dma("Q", lu.a(), up_h[:, o:o + 1024], writes=(lu.b,))
            P.dma("Q", lg.a(), g2_h[:, o:o + 1024], writes=(lg.b,))
            qdq_tile(env, UQ.a(), lu.a(), scl.a(1, 2), scl.a(0, 1),
                     -32768.0, 32767.0, (lu.b,), (UQ.b,), (scl.b,),
                     S1, S2, S3, 1024)
            st = ST[half]
            qdq_tile(env, st.a(), lg.a(), scl.a(3, 4), scl.a(2, 3),
                     -32768.0, 32767.0, (lg.b,), (st.b,), (scl.b,),
                     S1, S2, S3, 1024)
            P.v("tensor_tensor", st.a(), st.a(), UQ.a(), ALU.mult,
                reads=(UQ.b,), writes=(st.b,))
            P.v("tensor_reduce", mxt.a(0, 1), st.a(), axis=AX.X,
                op=ALU.max, apply_absolute_value=True,
                reads=(st.b,), writes=(mxt.b,))
            P.sep(mxt.b)
            P.v("tensor_tensor", RM.a(0, 1), RM.a(0, 1), mxt.a(0, 1),
                ALU.max, reads=(mxt.b,), writes=(RM.b,))
            P.sep(RM.b)
            env.store_ops.append(
                P.dma("Q", o_h[:, o:o + 1024], st.a(), reads=(st.b,)))
    mxo = AFr.alloc(16)
    env.pmax_scalar(RM.a(0, 1), (RM.b,), mxo.a(0, 1, 0, 1), (mxo.b,))
    env.store_ops.append(P.dma("Q", mx_h, mxo.a(0, 1, 0, 1),
                               reads=(mxo.b,)))
    env.finish()
    return env



def prog_I(nc):
    env = Env(nc)
    P, pp, AB, AFr = env.P, env.pp, env.AB, env.AF
    OC = 24
    o_h = env.uin("oraw", [128, OC * T])
    gmx_h = env.uin("gmx", [1, 8])
    w_dn = env.uin("L1_ffn_w_down", [1024, 3072])
    xres_h = env.uin("xres", [128, DC * T])
    xout_h = env.uout("xres_o", [128, DC * T])
    y_h = env.uout("y", [T, D])

    S1 = AFr.alloc(1024)
    S2 = AFr.alloc(1024)
    S3 = AFr.alloc(1024)
    MB = AFr.alloc(128)
    scl = make_scales(env, gmx_h, 1, 32767.0)
    OQ = AB.alloc(OC * T)
    LD = [AFr.alloc(1024) for _ in range(2)]
    for oc in range(OC):
        for half in range(2):
            o = oc * T + half * 1024
            ld = LD[half]
            P.dma("Q", ld.a(), o_h[:, o:o + 1024], writes=(ld.b,))
            qdq_tile(env, OQ.a(o, o + 1024), ld.a(), scl.a(1, 2),
                     scl.a(0, 1), -32768.0, 32767.0, (ld.b,), (OQ.b,),
                     (scl.b,), S1, S2, S3, 1024)
    for ld in LD:
        ld.free()

    xres = AFr.alloc(DC * T)
    P.dma("Q", xres.a(), xres_h, writes=(xres.b,))
    RAW = AFr.alloc(1024)
    WQB = AB.alloc(1024)
    WDT = [AB.alloc(OC * 128) for _ in range(2)]
    for dc in range(DC):
        wt = WDT[dc % 2]
        for pc in range(3):
            P.dma("Q", RAW.a(),
                  w_dn[dc * 128:(dc + 1) * 128,
                       pc * 1024:(pc + 1) * 1024], writes=(RAW.b,))
            dequant_piece(env, RAW.a(), (RAW.b,), WQB.a(), (WQB.b,),
                          1024, S1, S2, S3, MB)
            for j in range(8):
                P.dmat("Q", wt.a((pc * 8 + j) * 128, (pc * 8 + j + 1) * 128),
                       WQB.a(j * 128, (j + 1) * 128),
                       reads=(WQB.b,), writes=(wt.b,))
        for tg in range(TG):
            pt, pb = pp.nxt()
            for oc in range(OC):
                P.mm(pt[:], wt.a(oc * 128, (oc + 1) * 128),
                     OQ.a(oc * T + tg * 512, oc * T + tg * 512 + 512),
                     start=(oc == 0), stop=(oc == OC - 1),
                     reads=(wt.b, OQ.b), writes=(pb,))
            P.v("tensor_add",
                xres.a(dc * T + tg * 512, dc * T + tg * 512 + 512),
                xres.a(dc * T + tg * 512, dc * T + tg * 512 + 512),
                pt[:], reads=(pb,), writes=(xres.b,))
    env.store_ops.append(P.dma("Q", xout_h, xres.a(), reads=(xres.b,)))
    RAW.free()
    S1.free()
    S2.free()
    S3.free()
    MB.free()
    write_y_from_tile(env, xres, y_h)
    env.finish()
    return env



def prog_J(nc):
    env = Env(nc, iota=True)
    P, pp, AB, AFr = env.P, env.pp, env.AB, env.AF
    H = 8
    DH = 128
    wq_h = env.uin("L1_attn_b_Wq", [D, D])
    wk_h = env.uin("L1_attn_b_Wk", [D, D])
    wv_h = env.uin("L1_attn_b_Wv", [D, D])
    wg_h = env.uin("L1_attn_b_W_G", [D, D])
    wo_h = env.uin("L1_attn_b_W_O", [D, D])
    xres_h = env.uin("xres", [128, DC * T])
    xout_h = env.uout("xres_o", [128, DC * T])

    xt = AFr.alloc(DC * T)
    P.dma("Q", xt.a(), xres_h, writes=(xt.b,))
    xnb = AB.alloc(DC * T)
    env.layernorm(lambda c: xt.a(c * T, (c + 1) * T), (xt.b,),
                  lambda c: xnb.a(c * T, (c + 1) * T), (xnb.b,))
    xt.free()
    dbg_h = env.uout("jdbg", [128, 4 * T]) if JDBG else None

    IR8 = AFr.alloc(T)
    env.replicate_row(env.IR.a(0, T, 0, 1), (env.IR.b,), T, IR8, 0, p1=8)
    LG = AFr.alloc(16)
    P.s("activation", LG.a(0, 1, 0, 8), env.ICOL.a(0, 1, 0, 8), AF.Exp,
        scale=env.sc(-LN2, 0, 8), bias=env.sc(-5 * LN2, 0, 8),
        reads=(env.ICOL.b, env.SC.b), writes=(LG.b,))
    P.v("tensor_scalar", LG.a(1, 2, 0, 8), LG.a(0, 1, 0, 8), -1.0, 1.0,
        ALU.mult, ALU.add, reads=(), writes=(LG.b,))
    P.s("activation", LG.a(1, 2, 0, 8), LG.a(1, 2, 0, 8), AF.Ln,
        reads=(), writes=(LG.b,))
    GP = AFr.alloc(2 * T)
    P.v("tensor_scalar_mul", GP.a(0, T, 0, 8), IR8.a(0, T, 0, 8),
        LG.a(1, 2, 0, 8), reads=(IR8.b, LG.b), writes=(GP.b,))
    P.s("activation", GP.a(T, 2 * T, 0, 8), GP.a(0, T, 0, 8), AF.Exp,
        reads=(), writes=(GP.b,))
    P.s("activation", GP.a(0, T, 0, 8), GP.a(0, T, 0, 8), AF.Exp,
        scale=env.sc(-1.0, 0, 8), reads=(env.SC.b,), writes=(GP.b,))
    IR8.free()

    if JDBG:
        env.store_ops.append(
            P.dma("Q", dbg_h[:, 0:2 * T], GP.a(), reads=(GP.b,)))
        env.store_ops.append(
            P.dma("Q", dbg_h[:, 2 * T:2 * T + 16], LG.a(0, 16),
                  reads=(LG.b,)))
    IRR = AFr.alloc(512)
    env.replicate_row(env.IR.a(0, 512, 0, 1), (env.IR.b,), 512, IRR, 0)
    BND = AFr.alloc(4 * 512)
    FR = AFr.alloc(512)
    P.v("tensor_scalar", FR.a(0, 512), IRR.a(0, 512), env.ICOL.a(0, 1),
        None, ALU.subtract, reads=(IRR.b, env.ICOL.b), writes=(FR.b,))
    for i in range(4):
        P.v("tensor_single_scalar", BND.a(i * 512, (i + 1) * 512),
            FR.a(0, 512), float(i * 128), ALU.is_ge,
            reads=(FR.b,), writes=(BND.b,))
    FR.free()
    IRR.free()

    YG = AB.alloc(DC * T)
    GQR = AFr.alloc(T)
    GKR = AFr.alloc(T)
    RW = AFr.alloc(2 * T)
    QFh = AB.alloc(T)
    KFh = AB.alloc(T)
    VTh = AB.alloc(TT * 128)
    RSC = AFr.alloc(512)
    SQ = AFr.alloc(512)
    GF = AFr.alloc(512)
    GN = AFr.alloc(1024)
    RO = AFr.alloc(16)
    SMT = [AB.alloc(512) for _ in range(2)]
    wps = [AB.alloc(DC * 128) for _ in range(8)]
    for h in range(H):
        wq, wk, wv, wg = (wps[(h % 2) * 4 + i] for i in range(4))
        for kc in range(DC):
            for w_, hb in ((wq, wq_h), (wk, wk_h), (wv, wv_h), (wg, wg_h)):
                P.dma("P", w_.a(kc * 128, (kc + 1) * 128),
                      hb[kc * 128:(kc + 1) * 128,
                         h * 128:(h + 1) * 128], writes=(w_.b,))
        P.dma("Q", RW.a(0, T, 0, 1), GP.a(T, 2 * T, h, h + 1),
              reads=(GP.b,), writes=(RW.b,))
        P.dma("Q", RW.a(T, 2 * T, 0, 1), GP.a(0, T, h, h + 1),
              reads=(GP.b,), writes=(RW.b,))
        env.replicate_row(RW.a(0, T, 0, 1), (RW.b,), T, GQR, 0)
        env.replicate_row(RW.a(T, 2 * T, 0, 1), (RW.b,), T, GKR, 0)
        P.v("tensor_scalar_mul", GQR.a(), GQR.a(),
            float(1.0 / np.sqrt(DH)), reads=(), writes=(GQR.b,))
        for tg in range(TG):
            for wmat, dst, grep in ((wq, QFh, GQR), (wk, KFh, GKR)):
                pt, pb = pp.nxt()
                for kc in range(DC):
                    P.mm(pt[:], wmat.a(kc * 128, (kc + 1) * 128),
                         xnb.a(kc * T + tg * 512, kc * T + tg * 512 + 512),
                         start=(kc == 0), stop=(kc == DC - 1),
                         reads=(wmat.b, xnb.b), writes=(pb,))
                P.v("tensor_mul", dst.a(tg * 512, (tg + 1) * 512), pt[:],
                    grep.a(tg * 512, (tg + 1) * 512),
                    reads=(pb, grep.b), writes=(dst.b,))
        if JDBG and False:
            env.store_ops.append(
                P.dma("P", dbg_h[:, 0:T], QFh.a(0, T), reads=(QFh.b,)))
            env.store_ops.append(
                P.dma("P", dbg_h[:, T:2 * T], KFh.a(0, T), reads=(KFh.b,)))
            env.store_ops.append(
                P.dma("Q", dbg_h[:, 2 * T:3 * T], GQR.a(), reads=(GQR.b,)))
            env.store_ops.append(
                P.dma("Q", dbg_h[:, 3 * T:4 * T], GKR.a(), reads=(GKR.b,)))
        for tt in range(TT):
            pt, pb = pp.nxt()
            for kc in range(DC):
                P.mm(pt[:, 0:128],
                     xnb.a(kc * T + tt * 128, kc * T + tt * 128 + 128),
                     wv.a(kc * 128, (kc + 1) * 128),
                     start=(kc == 0), stop=(kc == DC - 1),
                     reads=(xnb.b, wv.b), writes=(pb,))
            P.v("tensor_copy", VTh.a(tt * 128, (tt + 1) * 128),
                pt[:, 0:128], reads=(pb,), writes=(VTh.b,))
        for ig in range(TG):
            pin = pp.pin(1)
            j_, pto, pob = pin[0]
            njt = 4 * (ig + 1)
            for jt in range(njt):
                ps, psb = pp.nxt()
                P.mm(ps[:], KFh.a(jt * 128, (jt + 1) * 128),
                     QFh.a(ig * 512, (ig + 1) * 512),
                     start=True, stop=True, reads=(KFh.b, QFh.b),
                     writes=(psb,))
                smt = SMT[jt % 2]
                if jt // 4 == ig:
                    bi = jt % 4
                    P.v("tensor_mul", smt.a(0, 512), ps[:],
                        BND.a(bi * 512, (bi + 1) * 512),
                        reads=(psb, BND.b), writes=(smt.b,))
                else:
                    P.v("tensor_copy", smt.a(0, 512), ps[:],
                        reads=(psb,), writes=(smt.b,))
                P.mm(pto[:], VTh.a(jt * 128, (jt + 1) * 128),
                     smt.a(0, 512), start=(jt == 0), stop=(jt == njt - 1),
                     reads=(VTh.b, smt.b), writes=(pob,))
            P.v("tensor_copy", RSC.a(0, 512), pto[:], reads=(pob,),
                writes=(RSC.b,))
            pp.unpin(pin)
            P.s("activation", SQ.a(0, 512), RSC.a(0, 512), AF.Square,
                reads=(RSC.b,), writes=(SQ.b,))
            p1, pb1 = pp.nxt()
            p2, pb2 = pp.nxt()
            P.mm(p1[0:1, :], env.ocf.a(0, 1), RSC.a(0, 512),
                 start=True, stop=True, reads=(env.ocf.b, RSC.b),
                 writes=(pb1,))
            P.mm(p2[0:1, :], env.ocf.a(0, 1), SQ.a(0, 512),
                 start=True, stop=True, reads=(env.ocf.b, SQ.b),
                 writes=(pb2,))
            P.v("tensor_scalar_mul", GF.a(0, 512, 0, 1), p1[0:1, :],
                1.0 / 128.0, reads=(pb1,), writes=(GF.b,))
            P.v("tensor_scalar_mul", GN.a(0, 512, 0, 1), p2[0:1, :],
                1.0 / 128.0, reads=(pb2,), writes=(GN.b,))
            P.v("scalar_tensor_tensor", GN.a(512, 1024, 0, 1),
                GF.a(0, 512, 0, 1), -1.0, GF.a(0, 512, 0, 1),
                ALU.mult, ALU.mult, reads=(GF.b,), writes=(GN.b,))
            P.v("tensor_add", GN.a(0, 512, 0, 1),
                GN.a(0, 512, 0, 1), GN.a(512, 1024, 0, 1),
                reads=(), writes=(GN.b,))
            P.s("activation", GN.a(0, 512, 0, 1), GN.a(0, 512, 0, 1),
                AF.Sqrt, bias=env.sc(1e-6, 0, 1), reads=(env.SC.b,),
                writes=(GN.b,))
            P.v("reciprocal", GN.a(0, 512, 0, 1), GN.a(0, 512, 0, 1),
                reads=(), writes=(GN.b,))
            pm, pmb = pp.nxt()
            pr, prb = pp.nxt()
            P.mm(pm[:], env.orow.a(0, 128, 0, 1), GF.a(0, 512, 0, 1),
                 start=True, stop=True, reads=(env.orow.b, GF.b),
                 writes=(pmb,))
            P.mm(pr[:], env.orow.a(0, 128, 0, 1), GN.a(0, 512, 0, 1),
                 start=True, stop=True, reads=(env.orow.b, GN.b),
                 writes=(prb,))
            P.v("tensor_tensor", RSC.a(0, 512), RSC.a(0, 512), pm[:],
                ALU.subtract, reads=(pmb,), writes=(RSC.b,))
            P.v("tensor_tensor", RSC.a(0, 512), RSC.a(0, 512), pr[:],
                ALU.mult, reads=(prb,), writes=(RSC.b,))
            pg, pgb = pp.nxt()
            for kc in range(DC):
                P.mm(pg[:], wg.a(kc * 128, (kc + 1) * 128),
                     xnb.a(kc * T + ig * 512, kc * T + ig * 512 + 512),
                     start=(kc == 0), stop=(kc == DC - 1),
                     reads=(wg.b, xnb.b), writes=(pgb,))
            P.s("activation", GF.a(0, 512), pg[:], AF.Silu,
                reads=(pgb,), writes=(GF.b,))
            P.v("tensor_tensor", YG.a(h * T + ig * 512, h * T + ig * 512 + 512),
                RSC.a(0, 512), GF.a(0, 512), ALU.mult,
                reads=(RSC.b, GF.b), writes=(YG.b,))
    scr = AFr.alloc(512)
    wos = [AB.alloc(DC * 128) for _ in range(2)]
    for dc in range(DC):
        wo = wos[dc % 2]
        for oc in range(DC):
            P.dma("P", wo.a(oc * 128, (oc + 1) * 128),
                  wo_h[oc * 128:(oc + 1) * 128, dc * 128:(dc + 1) * 128],
                  writes=(wo.b,))
        for tg in range(TG):
            pt, pb = pp.nxt()
            for oc in range(DC):
                P.mm(pt[:], wo.a(oc * 128, (oc + 1) * 128),
                     YG.a(oc * T + tg * 512, oc * T + tg * 512 + 512),
                     start=(oc == 0), stop=(oc == DC - 1),
                     reads=(wo.b, YG.b), writes=(pb,))
            stream_add_spill(env, dc, tg, pt[:], pb, xres_h, xout_h, None,
                             scr)
    env.finish()
    return env



def prog_B1(nc):
    env = Env(nc, iota=True)
    P, pp, AB, AFr = env.P, env.pp, env.AB, env.AF
    DH = 64
    W = 256
    INNER = 1024
    DI = 2730
    NP = (DI + 127) // 128
    w_qkv = env.uin("Wqkv", [D, 3 * INNER])
    q_scale = env.uin("q_scale", [DH])
    k_scale = env.uin("k_scale", [DH])
    w_o = env.uin("Wo", [INNER, D])
    ln_g = env.uin("ln_g", [D])
    ln_b = env.uin("ln_b", [D])
    w1 = env.uin("W1", [D, 2 * DI])
    b1 = env.uin("b1", [2 * DI])
    w2 = env.uin("W2", [DI, D])
    b2 = env.uin("b2", [D])
    xloc_h = env.uin("xloc", [128, DC * T])
    xout_h = env.uout("xloc_o", [128, DC * T])

    stride = T + 4
    xl = AFr.alloc(DC * stride)
    for c in range(DC):
        P.v("memset", xl.a(c * stride, c * stride + 4), 0.0, writes=(xl.b,))
        P.dma("Q", xl.a(c * stride + 4, c * stride + 4 + T),
              xloc_h[:, c * T:(c + 1) * T], writes=(xl.b,))

    def xlc(c, t0=0, t1=T):
        return xl.a(c * stride + 4 + t0, c * stride + 4 + t1)

    def xlc_sh(c, t0=0, t1=T):
        o = 3 if c >= 4 else 4
        return xl.a(c * stride + o + t0, c * stride + o + t1)

    sxb = AB.alloc(DC * T)
    for c in range(DC):
        P.s("copy", sxb.a(c * T, (c + 1) * T), xlc_sh(c),
            reads=(xl.b,), writes=(sxb.b,))

    qk = AFr.alloc(2 * DH)
    P.dma("Q", qk.a(0, DH, 0, 1), q_scale.unsqueeze(0), writes=(qk.b,))
    P.dma("Q", qk.a(DH, 2 * DH, 0, 1), k_scale.unsqueeze(0), writes=(qk.b,))
    P.v("tensor_scalar_mul", qk.a(0, DH, 0, 1), qk.a(0, DH, 0, 1), 8.0,
        reads=(), writes=(qk.b,))
    QSR = AFr.alloc(2 * DH)
    env.replicate_row(qk.a(0, DH, 0, 1), (qk.b,), DH, QSR, 0)
    env.replicate_row(qk.a(DH, 2 * DH, 0, 1), (qk.b,), DH, QSR, DH)
    qk.free()

    def build_tables():
        TBL = AB.alloc(TT * 4 * DH)
        IVF = AFr.alloc(2 * DH)
        tmp = AFr.alloc(2 * DH)
        P.v("tensor_scalar_mul", tmp.a(0, 32, 0, 1), env.IR.a(0, 32, 0, 1),
            -float(np.log(10000.0) / 32.0), reads=(env.IR.b,),
            writes=(tmp.b,))
        P.s("activation", tmp.a(0, 32, 0, 1), tmp.a(0, 32, 0, 1), AF.Exp,
            reads=(), writes=(tmp.b,))
        P.v("tensor_copy", tmp.a(32, 64, 0, 1), tmp.a(0, 32, 0, 1),
            reads=(), writes=(tmp.b,))
        P.v("tensor_scalar", tmp.a(64, 96, 0, 1), env.IR.a(0, 32, 0, 1),
            2.0 / 89.6, 25.6 / 89.6, ALU.mult, ALU.add,
            reads=(env.IR.b,), writes=(tmp.b,))
        P.s("activation", tmp.a(64, 96, 0, 1), tmp.a(64, 96, 0, 1), AF.Ln,
            reads=(), writes=(tmp.b,))
        P.v("tensor_copy", tmp.a(96, 128, 0, 1), tmp.a(64, 96, 0, 1),
            reads=(), writes=(tmp.b,))
        env.replicate_row(tmp.a(0, 64, 0, 1), (tmp.b,), DH, IVF, 0)
        env.replicate_row(tmp.a(64, 128, 0, 1), (tmp.b,), DH, IVF, DH)
        tmp.free()
        ts = AFr.alloc(6 * DH)
        tcol = AFr.alloc(16)
        for tt in range(TT):
            P.v("tensor_scalar_add", tcol.a(0, 1), env.ICOL.a(0, 1),
                float(128 * tt), reads=(env.ICOL.b,), writes=(tcol.b,))
            P.sep(tcol.b)
            P.v("tensor_scalar", tcol.a(1, 2), env.ICOL.a(0, 1),
                1.0 / 128.0, float((128 * tt - 1024) / 128.0),
                ALU.mult, ALU.add, reads=(env.ICOL.b,), writes=(tcol.b,))
            P.sep(tcol.b)
            P.v("tensor_scalar_mul", ts.a(0, DH), IVF.a(0, DH),
                tcol.a(0, 1), reads=(IVF.b, tcol.b), writes=(ts.b,))
            P.sep(ts.b)
            P.v("tensor_scalar_mul", ts.a(DH, 2 * DH), ts.a(0, DH),
                float(1.0 / (2 * np.pi)), reads=(), writes=(ts.b,))
            P.sep(ts.b)
            P.v("tensor_copy", env.IT[:, 0:DH], ts.a(DH, 2 * DH),
                reads=(ts.b,), writes=(env.ITb,))
            P.v("tensor_copy", ts.a(DH, 2 * DH), env.IT[:, 0:DH],
                reads=(env.ITb,), writes=(ts.b,))
            P.sep(ts.b)
            P.v("scalar_tensor_tensor", ts.a(0, DH), ts.a(DH, 2 * DH),
                -float(TWO_PI_HI), ts.a(0, DH), ALU.mult, ALU.add,
                reads=(), writes=(ts.b,))
            P.sep(ts.b)
            P.v("scalar_tensor_tensor", ts.a(0, DH), ts.a(DH, 2 * DH),
                -float(TWO_PI_LO), ts.a(0, DH), ALU.mult, ALU.add,
                reads=(), writes=(ts.b,))
            P.sep(ts.b)
            P.s("activation", ts.a(DH, 2 * DH), ts.a(0, DH), AF.Sin,
                reads=(), writes=(ts.b,))
            P.sep(ts.b)
            P.s("activation", ts.a(2 * DH, 3 * DH), ts.a(0, DH), AF.Sin,
                scale=env.sc(0.5), reads=(env.SC.b,), writes=(ts.b,))
            P.sep(ts.b)
            P.s("activation", ts.a(2 * DH, 3 * DH), ts.a(2 * DH, 3 * DH),
                AF.Square, reads=(), writes=(ts.b,))
            P.sep(ts.b)
            P.v("tensor_scalar", ts.a(2 * DH, 3 * DH), ts.a(2 * DH, 3 * DH),
                -2.0, 1.0, ALU.mult, ALU.add, reads=(), writes=(ts.b,))
            P.sep(ts.b)
            P.v("tensor_scalar_mul", ts.a(3 * DH, 4 * DH), IVF.a(DH, 2 * DH),
                tcol.a(1, 2), reads=(IVF.b, tcol.b), writes=(ts.b,))
            P.sep(ts.b)
            P.s("activation", ts.a(4 * DH, 5 * DH), ts.a(3 * DH, 4 * DH),
                AF.Exp, reads=(), writes=(ts.b,))
            P.sep(ts.b)
            P.s("activation", ts.a(5 * DH, 6 * DH), ts.a(3 * DH, 4 * DH),
                AF.Exp, scale=env.sc(-1.0), reads=(env.SC.b,),
                writes=(ts.b,))
            P.sep(ts.b)
            tb = tt * 4 * DH
            P.v("tensor_mul", TBL.a(tb, tb + DH), ts.a(2 * DH, 3 * DH),
                ts.a(4 * DH, 5 * DH), reads=(ts.b,), writes=(TBL.b,))
            P.sep(TBL.b)
            P.v("tensor_mul", TBL.a(tb + DH, tb + 2 * DH), ts.a(DH, 2 * DH),
                ts.a(4 * DH, 5 * DH), reads=(ts.b,), writes=(TBL.b,))
            P.sep(TBL.b)
            P.v("tensor_mul", TBL.a(tb + 2 * DH, tb + 3 * DH),
                ts.a(2 * DH, 3 * DH), ts.a(5 * DH, 6 * DH),
                reads=(ts.b,), writes=(TBL.b,))
            P.sep(TBL.b)
            P.v("tensor_mul", TBL.a(tb + 3 * DH, tb + 4 * DH),
                ts.a(DH, 2 * DH), ts.a(5 * DH, 6 * DH),
                reads=(ts.b,), writes=(TBL.b,))
            P.sep(TBL.b)
        ts.free()
        tcol.free()
        IVF.free()
        return TBL

    IRR = AFr.alloc(512)
    env.replicate_row(env.IR.a(0, 512, 0, 1), (env.IR.b,), 512, IRR, 0)
    MSK = AB.alloc(4 * 512)
    FR = AFr.alloc(512)
    M2 = AFr.alloc(512)
    M3 = AFr.alloc(512)
    P.v("tensor_scalar", FR.a(0, 512), IRR.a(0, 512), env.ICOL.a(0, 1),
        None, ALU.subtract, reads=(IRR.b, env.ICOL.b), writes=(FR.b,))
    for i, (off, w0) in enumerate(((256, False), (384, False),
                                   (256, True), (384, True))):
        P.v("tensor_single_scalar", M3.a(0, 512),
            FR.a(0, 512), float(off), ALU.is_gt,
            reads=(FR.b,), writes=(M3.b,))
        P.v("tensor_single_scalar", M2.a(0, 512), FR.a(0, 512),
            float(off - W), ALU.is_lt, reads=(FR.b,), writes=(M2.b,))
        P.v("tensor_add", M3.a(0, 512), M3.a(0, 512), M2.a(0, 512),
            reads=(M2.b,), writes=(M3.b,))
        if w0:
            P.v("tensor_single_scalar", M2.a(0, 512), IRR.a(0, 512),
                256.0, ALU.is_lt, reads=(IRR.b,), writes=(M2.b,))
            P.v("tensor_add", M3.a(0, 512), M3.a(0, 512), M2.a(0, 512),
                reads=(M2.b,), writes=(M3.b,))
        P.v("tensor_scalar_mul", MSK.a(i * 512, (i + 1) * 512),
            M3.a(0, 512), -1e30, reads=(M3.b,), writes=(MSK.b,))
    FR.free()
    M2.free()
    M3.free()
    IRR.free()

    QF = AB.alloc(4 * T)
    KF = AB.alloc(4 * T)
    VT = AB.alloc(TT * 512)
    QT = AFr.alloc(128)
    QR = AFr.alloc(128)
    SMAX = AFr.alloc(16)
    QSTG = AB.alloc(128)
    ES = AB.alloc(512)
    ES2 = AB.alloc(512)
    SCR = AFr.alloc(512)
    for hg in range(2):
        hc0 = hg * 512
        attnout = AB.alloc(4 * T)
        TBL = build_tables()
        for part in range(3):
            dstf = (QF, KF, None)[part]
            wsl = [AB.alloc(DC * 128) for _ in range(2)]
            for og in range(4):
                ws = wsl[og % 2]
                for kc in range(DC):
                    P.dma("P", ws.a(kc * 128, (kc + 1) * 128),
                          w_qkv[kc * 128:(kc + 1) * 128,
                                part * 1024 + hc0 + og * 128:
                                part * 1024 + hc0 + og * 128 + 128],
                          writes=(ws.b,))
                for tt in range(TT):
                    pt, pb = pp.nxt()
                    for kc in range(DC):
                        P.mm(pt[:, 0:128],
                             sxb.a(kc * T + tt * 128,
                                   kc * T + tt * 128 + 128),
                             ws.a(kc * 128, (kc + 1) * 128),
                             start=(kc == 0), stop=(kc == DC - 1),
                             reads=(sxb.b, ws.b), writes=(pb,))
                    if part == 2:
                        P.v("tensor_copy",
                            VT.a(tt * 512 + og * 128,
                                 tt * 512 + og * 128 + 128),
                            pt[:, 0:128], reads=(pb,), writes=(VT.b,))
                        continue
                    P.v("tensor_copy", QT.a(0, 128), pt[:, 0:128],
                        reads=(pb,), writes=(QT.b,))
                    P.s("activation", SCR.a(0, 128), QT.a(0, 128), AF.Square,
                        reads=(QT.b,), writes=(SCR.b,))
                    P.v("tensor_reduce", SMAX.a(0, 2),
                        SCR.a(0, 128).rearrange("p (h e) -> p h e", e=DH),
                        axis=AX.X, op=ALU.add, reads=(SCR.b,),
                        writes=(SMAX.b,))
                    P.s("activation", SMAX.a(0, 2), SMAX.a(0, 2), AF.Sqrt,
                        reads=(), writes=(SMAX.b,))
                    P.v("tensor_scalar_max", SMAX.a(0, 2), SMAX.a(0, 2),
                        1e-12, reads=(), writes=(SMAX.b,))
                    P.sep(SMAX.b)
                    P.v("reciprocal", SMAX.a(0, 2), SMAX.a(0, 2),
                        reads=(), writes=(SMAX.b,))
                    P.sep(SMAX.b)
                    P.v("tensor_tensor", QT.a(0, 128), QT.a(0, 128),
                        SMAX.a(0, 2).unsqueeze(2).broadcast_to([128, 2, DH]),
                        ALU.mult, reads=(SMAX.b,), writes=(QT.b,))
                    P.v("tensor_tensor", QT.a(0, 128), QT.a(0, 128),
                        QSR.a(part * DH, part * DH + DH).unsqueeze(1)
                        .broadcast_to([128, 2, DH]),
                        ALU.mult, reads=(QSR.b,), writes=(QT.b,))
                    tb = tt * 4 * DH + part * 2 * DH
                    P.v("tensor_tensor", QR.a(0, 128), QT.a(0, 128),
                        TBL.a(tb, tb + DH).unsqueeze(1)
                        .broadcast_to([128, 2, DH]),
                        ALU.mult, reads=(QT.b, TBL.b), writes=(QR.b,))
                    for h2 in range(2):
                        o = h2 * DH
                        P.v("tensor_tensor", SCR.a(o, o + 32),
                            QT.a(o + 32, o + 64),
                            TBL.a(tb + DH, tb + DH + 32), ALU.mult,
                            reads=(QT.b, TBL.b), writes=(SCR.b,))
                        P.v("tensor_tensor", SCR.a(o + 32, o + 64),
                            QT.a(o, o + 32),
                            TBL.a(tb + DH + 32, tb + 2 * DH), ALU.mult,
                            reads=(QT.b, TBL.b), writes=(SCR.b,))
                        P.v("scalar_tensor_tensor", QR.a(o, o + 32),
                            SCR.a(o, o + 32), -1.0, QR.a(o, o + 32),
                            ALU.mult, ALU.add, reads=(SCR.b,),
                            writes=(QR.b,))
                        P.v("tensor_add", QR.a(o + 32, o + 64),
                            QR.a(o + 32, o + 64), SCR.a(o + 32, o + 64),
                            reads=(SCR.b,), writes=(QR.b,))
                    P.s("copy", QSTG.a(0, 128), QR.a(0, 128),
                        reads=(QR.b,), writes=(QSTG.b,))
                    P.dmat("Q",
                           dstf.a(og * T + tt * 128,
                                  og * T + tt * 128 + 128),
                           QSTG.a(0, 128), reads=(QSTG.b,),
                           writes=(dstf.b,))
            for ws in wsl:
                ws.free()
        for h2 in range(8):
            qg = h2 // 2
            p0 = (h2 % 2) * 64
            for it in range(TT):
                w = it // 2
                pt, pb = pp.nxt()
                j0 = (w - 1) * W
                if w == 0:
                    P.mm(pt[:, 256:512],
                         QF.a(qg * T + it * 128, qg * T + it * 128 + 128,
                              p0, p0 + 64),
                         KF.a(qg * T, qg * T + 256, p0, p0 + 64),
                         start=True, stop=True, reads=(QF.b, KF.b),
                         writes=(pb,), tp=(p0, 0))
                else:
                    P.mm(pt[:],
                         QF.a(qg * T + it * 128, qg * T + it * 128 + 128,
                              p0, p0 + 64),
                         KF.a(qg * T + j0, qg * T + j0 + 512, p0, p0 + 64),
                         start=True, stop=True, reads=(QF.b, KF.b),
                         writes=(pb,), tp=(p0, 0))
                mi = (it % 2) + (2 if w == 0 else 0)
                P.v("tensor_tensor", SCR.a(0, 512), pt[:],
                    MSK.a(mi * 512, (mi + 1) * 512), ALU.add,
                    reads=(pb, MSK.b), writes=(SCR.b,))
                P.v("tensor_reduce", SMAX.a(0, 1), SCR.a(0, 512), axis=AX.X,
                    op=ALU.max, reads=(SCR.b,), writes=(SMAX.b,))
                P.sep(SMAX.b)
                P.v("tensor_scalar_mul", SMAX.a(0, 1), SMAX.a(0, 1), -1.0,
                    reads=(), writes=(SMAX.b,))
                P.sep(SMAX.b)
                P.s("activation", ES.a(0, 512), SCR.a(0, 512), AF.Exp,
                    bias=SMAX.a(0, 1), reads=(SCR.b, SMAX.b),
                    writes=(ES.b,))
                P.v("tensor_reduce", SMAX.a(1, 2), ES.a(0, 512), axis=AX.X,
                    op=ALU.add, reads=(ES.b,), writes=(SMAX.b,))
                P.sep(SMAX.b)
                P.v("reciprocal", SMAX.a(1, 2), SMAX.a(1, 2), reads=(),
                    writes=(SMAX.b,))
                P.sep(SMAX.b)
                P.v("tensor_scalar_mul", ES2.a(0, 512), ES.a(0, 512),
                    SMAX.a(1, 2), reads=(ES.b, SMAX.b), writes=(ES2.b,))
                po, pob = pp.nxt()
                ATT = [None, None]
                first = True
                for jt in range(4):
                    if w == 0 and jt < 2:
                        continue
                    at = ATT[jt % 2]
                    if at is None:
                        at = AB.alloc(128)
                        ATT[jt % 2] = at
                    P.dmat("Q", at.a(0, 128),
                           ES2.a(jt * 128, jt * 128 + 128),
                           reads=(ES2.b,), writes=(at.b,))
                    jtile = (w - 1) * 2 + jt
                    vcol = jtile * 512 + qg * 128 + p0
                    P.mm(po[p0:p0 + 64, 0:128],
                         VT.a(vcol, vcol + 64),
                         at.a(0, 128), start=first, stop=(jt == 3),
                         reads=(VT.b, at.b), writes=(pob,),
                         tp=(0, p0))
                    first = False
                for at in ATT:
                    if at is not None:
                        at.free()
                P.v("tensor_copy",
                    attnout.a((h2 // 2) * T + it * 128,
                              (h2 // 2) * T + it * 128 + 128,
                              p0, p0 + 64),
                    po[p0:p0 + 64, 0:128], reads=(pob,),
                    writes=(attnout.b,))
        wos = [AB.alloc(4 * 128) for _ in range(2)]
        for dc in range(DC):
            wo = wos[dc % 2]
            for oc in range(4):
                P.dma("P", wo.a(oc * 128, (oc + 1) * 128),
                      w_o[(hg * 4 + oc) * 128:(hg * 4 + oc + 1) * 128,
                          dc * 128:(dc + 1) * 128], writes=(wo.b,))
            for tg in range(TG):
                pt, pb = pp.nxt()
                for oc in range(4):
                    P.mm(pt[:], wo.a(oc * 128, (oc + 1) * 128),
                         attnout.a(oc * T + tg * 512,
                                   oc * T + tg * 512 + 512),
                         start=(oc == 0), stop=(oc == 3),
                         reads=(wo.b, attnout.b), writes=(pb,))
                P.v("tensor_add", xlc(dc, tg * 512, tg * 512 + 512),
                    xlc(dc, tg * 512, tg * 512 + 512), pt[:],
                    reads=(pb,), writes=(xl.b,))
        for wo in wos:
            wo.free()
        attnout.free()
        TBL.free()
    QF.free()
    KF.free()
    VT.free()
    sxb.free()
    QSR.free()
    MSK.free()
    QT.free()
    QR.free()
    QSTG.free()
    ES.free()
    ES2.free()
    SCR.free()
    P.barrier()

    LNG = AFr.alloc(DC)
    LNB = AFr.alloc(DC)
    P.dma("Q", LNG.a(0, DC), ln_g.rearrange("(c p) -> p c", p=128),
          writes=(LNG.b,), slow=True)
    P.dma("Q", LNB.a(0, DC), ln_b.rearrange("(c p) -> p c", p=128),
          writes=(LNB.b,), slow=True)
    xnb = AB.alloc(DC * T)
    env.layernorm(lambda c: xlc_sh(c), (xl.b,),
                  lambda c: xnb.a(c * T, (c + 1) * T), (xnb.b,),
                  gcol=(LNG, 0), bcol=(LNB, 0))
    B1C = AFr.alloc(2 * NP)
    B2C = AFr.alloc(DC)
    for p in range(NP):
        wp_ = min(128, DI - p * 128)
        P.dma("Q", B1C.a(p, p + 1, 0, wp_),
              b1[p * 128:p * 128 + wp_].unsqueeze(1), writes=(B1C.b,))
        P.dma("Q", B1C.a(NP + p, NP + p + 1, 0, wp_),
              b1[DI + p * 128:DI + p * 128 + wp_].unsqueeze(1),
              writes=(B1C.b,))
    P.dma("Q", B2C.a(0, DC), b2.rearrange("(c p) -> p c", p=128),
          writes=(B2C.b,), slow=True)

    NH = (NP + 1) // 2
    MP = AB.alloc(NH * T)
    GEL = AFr.alloc(512)
    for half in range(2):
        p0h, p1h = half * NH, min(NP, (half + 1) * NH)
        w1s = [AB.alloc(DC * 128) for _ in range(4)]
        for p in range(p0h, p1h):
            lp = p - p0h
            wp_ = min(128, DI - p * 128)
            wa = w1s[2 * (p % 2)]
            wg = w1s[2 * (p % 2) + 1]
            for kc in range(DC):
                P.dma("P", wa.a(kc * 128, kc * 128 + wp_),
                      w1[kc * 128:(kc + 1) * 128, p * 128:p * 128 + wp_],
                      writes=(wa.b,))
                P.dma("P", wg.a(kc * 128, kc * 128 + wp_),
                      w1[kc * 128:(kc + 1) * 128,
                         DI + p * 128:DI + p * 128 + wp_], writes=(wg.b,))
            for tg in range(TG):
                pa, pab = pp.nxt()
                pg, pgb = pp.nxt()
                for kc in range(DC):
                    P.mm(pa[0:wp_, :], wa.a(kc * 128, kc * 128 + wp_),
                         xnb.a(kc * T + tg * 512, kc * T + tg * 512 + 512),
                         start=(kc == 0), stop=(kc == DC - 1),
                         reads=(wa.b, xnb.b), writes=(pab,))
                for kc in range(DC):
                    P.mm(pg[0:wp_, :], wg.a(kc * 128, kc * 128 + wp_),
                         xnb.a(kc * T + tg * 512, kc * T + tg * 512 + 512),
                         start=(kc == 0), stop=(kc == DC - 1),
                         reads=(wg.b, xnb.b), writes=(pgb,))
                P.s("activation", GEL.a(0, 512, 0, wp_), pg[0:wp_, :],
                    AF.Gelu, bias=B1C.a(NP + p, NP + p + 1, 0, wp_),
                    reads=(pgb, B1C.b), writes=(GEL.b,))
                P.v("scalar_tensor_tensor",
                    MP.a(lp * T + tg * 512, lp * T + tg * 512 + 512,
                         0, wp_),
                    pa[0:wp_, :], B1C.a(p, p + 1, 0, wp_),
                    GEL.a(0, 512, 0, wp_), ALU.add, ALU.mult,
                    reads=(pab, B1C.b, GEL.b), writes=(MP.b,))
        for w_ in w1s:
            w_.free()
        w2s = [AB.alloc(NH * 128) for _ in range(2)]
        for dc in range(DC):
            ws = w2s[dc % 2]
            for p in range(p0h, p1h):
                lp = p - p0h
                wp_ = min(128, DI - p * 128)
                P.dma("P", ws.a(lp * 128, (lp + 1) * 128, 0, wp_),
                      w2[p * 128:p * 128 + wp_, dc * 128:(dc + 1) * 128],
                      writes=(ws.b,))
            for tg in range(TG):
                pt, pb = pp.nxt()
                for p in range(p0h, p1h):
                    lp = p - p0h
                    wp_ = min(128, DI - p * 128)
                    P.mm(pt[:], ws.a(lp * 128, (lp + 1) * 128, 0, wp_),
                         MP.a(lp * T + tg * 512, lp * T + tg * 512 + 512,
                              0, wp_),
                         start=(p == p0h), stop=(p == p1h - 1),
                         reads=(ws.b, MP.b), writes=(pb,))
                if half == 0:
                    P.v("scalar_tensor_tensor",
                        xlc(dc, tg * 512, tg * 512 + 512),
                        pt[:], B2C.a(dc, dc + 1),
                        xlc(dc, tg * 512, tg * 512 + 512),
                        ALU.add, ALU.add, reads=(pb, B2C.b),
                        writes=(xl.b,))
                else:
                    P.v("tensor_add", xlc(dc, tg * 512, tg * 512 + 512),
                        xlc(dc, tg * 512, tg * 512 + 512), pt[:],
                        reads=(pb,), writes=(xl.b,))
        for ws in w2s:
            ws.free()
    xnb.free()
    GEL.free()
    MP.free()
    for c in range(DC):
        env.store_ops.append(
            P.dma("Q", xout_h[:, c * T:(c + 1) * T], xlc(c),
                  reads=(xl.b,)))
    env.finish()
    return env



BUILDERS = {"A": prog_A, "E": prog_E, "P": prog_P, "O": prog_O,
            "C": lambda nc: prog_mamba(nc, False),
            "D": lambda nc: prog_mamba(nc, True),
            "B1": prog_B1, "F": prog_F, "G": prog_G, "H": prog_H,
            "I": prog_I, "J": prog_J}

_PROG_CACHE = {}


def get_prog(name):
    if name not in _PROG_CACHE:
        nc = bass.Bass("TRN2", target_bir_lowering=False, debug=False)
        env = BUILDERS[name](nc)
        _PROG_CACHE[name] = (nc, list(env.used_inputs), list(env.out_names))
    return _PROG_CACHE[name]


def make_consts():
    bf = ml_dtypes.bfloat16
    return {
        "c_ident": np.eye(128, dtype=np.float32),
        "c_ones_col_b": np.ones((128, 1), dtype=bf),
        "c_ones_col_f": np.ones((128, 1), dtype=np.float32),
        "c_ones_row": np.ones((1, 128), dtype=np.float32),
        "c_ones_row_b": np.ones((1, 128), dtype=bf),
        "c_scalars": np.tile(np.array([1e-5, 1.0, -1.0, 1e-6, 0.0, 0.5,
                                       1e-12, 2.0, -LN2, -5 * LN2, 0, 0],
                                      dtype=np.float32), (128, 1)),
        "c_iota_row": np.arange(T, dtype=np.float32).reshape(1, T),
        "c_iota_col": np.arange(128, dtype=np.float32).reshape(128, 1),
    }


def kernel(**inputs):
    consts = make_consts()
    st = [dict() for _ in range(NCORES)]

    def launch(name, extra=None):
        nc, used, outs = get_prog(name)
        maps = []
        for c in range(NCORES):
            m = {}
            for n in used:
                if extra and n in extra:
                    v = extra[n]
                    m[n] = v(c) if callable(v) else v
                elif n.startswith("c_"):
                    m[n] = consts[n]
                elif n in ("xres", "xloc", "xnq", "upraw", "gateraw",
                           "g2raw", "oraw"):
                    m[n] = st[c][n]
                else:
                    m[n] = np.ascontiguousarray(inputs[n])
            maps.append(m)
        res = run_bass_kernel_spmd(nc, maps, core_ids=list(range(NCORES)))
        for c in range(NCORES):
            for o in outs:
                key = {"xres_o": "xres", "xloc_o": "xloc"}.get(o, o)
                st[c][key] = np.asarray(res.results[c][o])

    def gmx_of(key, cols):
        vals = []
        for (k, ci) in cols:
            vals.extend(np.asarray(st[c][k]).reshape(-1)[ci]
                        for c in range(NCORES))
        return np.array(vals, dtype=np.float32).reshape(1, -1)

    steps = list(STEPS)
    assert 1 in steps, "dev subsets must include step 1"
    launch("A", extra={"L0_attn_a_x":
                       (lambda c: np.ascontiguousarray(
                           inputs["L0_attn_a_x"][c]))})

    def run_l0ffn():
        launch("E")
        for c in range(NCORES):
            st[c]["xloc"] = st[c]["xnq"]
        for i in range(6):
            launch("B1", extra={
                "Wqkv": np.ascontiguousarray(inputs["L0_ffn_Wqkv"][i]),
                "q_scale": np.ascontiguousarray(inputs["L0_ffn_q_scale"][i]),
                "k_scale": np.ascontiguousarray(inputs["L0_ffn_k_scale"][i]),
                "Wo": np.ascontiguousarray(inputs["L0_ffn_Wo"][i]),
                "ln_g": np.ascontiguousarray(inputs["L0_ffn_ln_g"][i]),
                "ln_b": np.ascontiguousarray(inputs["L0_ffn_ln_b"][i]),
                "W1": np.ascontiguousarray(inputs["L0_ffn_W1"][i]),
                "b1": np.ascontiguousarray(inputs["L0_ffn_b1"][i]),
                "W2": np.ascontiguousarray(inputs["L0_ffn_W2"][i]),
                "b2": np.ascontiguousarray(inputs["L0_ffn_b2"][i]),
            })
        launch("P")

    def run_l1ffn():
        launch("E")
        launch("F", extra={"gmx": gmx_of("mx", [("mx", 0)])})
        mxf = [np.asarray(st[c]["mx"]).reshape(-1).copy()
               for c in range(NCORES)]
        for c in range(NCORES):
            st[c]["_mxf"] = mxf[c]
        launch("G", extra={"gmx": gmx_of("_mxf", [("_mxf", 1)])})
        launch("H", extra={"gmx": np.concatenate(
            [gmx_of("_mxf", [("_mxf", 0)]),
             gmx_of("mx", [("mx", 0)])], axis=1)})
        launch("I", extra={"gmx": gmx_of("mx", [("mx", 0)])})

    if 2 in steps:
        run_l0ffn()
    if 3 in steps:
        launch("C")
    if 4 in steps:
        run_l0ffn()
    if 5 in steps:
        launch("D")
    if 6 in steps:
        run_l1ffn()
    if 7 in steps:
        launch("J")
    if 8 in steps:
        run_l1ffn()
    if "y" not in st[0] or steps[-1] != 8 or 8 not in steps:
        if 8 not in steps:
            launch("O")
    return np.stack([st[c]["y"] for c in range(NCORES)], axis=0)
